# version 13
# baseline (speedup 1.0000x reference)
import math
import numpy as np
from contextlib import ExitStack

import concourse.bass as bass
import concourse.mybir as mybir
from concourse.bass_utils import run_bass_kernel_spmd

F32 = mybir.dt.float32
BF16 = mybir.dt.bfloat16
AX = mybir.AxisListType.X
AF = mybir.ActivationFunctionType
ALU = mybir.AluOpType

P = 128
T = 2048
D = 2048
KT = 16
NT = 16
H = 16
CH = 4
E = 8
ISQ = 1.0 / math.sqrt(128.0)
EPS = 1e-5
NEG = -1.0e9


class Q:

    def __init__(self, eng, sem):
        self.eng = eng
        self.sem = sem
        self.n = 0
        self.waited = {}

    def inc(self, ins, k=1):
        self.n += k
        ins.then_inc(self.sem, k)
        return self.n

    def wait(self, sem, val):
        key = id(sem)
        if self.waited.get(key, -1) >= val:
            return
        self.eng.wait_ge(sem, val)
        self.waited[key] = val


class Ev:

    def __init__(self):
        self.d = {}

    def put(self, name, q):
        self.d[name] = (q.sem, q.n)

    def wait(self, q, name):
        s, v = self.d[name]
        q.wait(s, v)

    def has(self, name):
        return name in self.d


def build(debug=False):
    nc = bass.Bass("TRN2", target_bir_lowering=False)

    xT = nc.dram_tensor("xT", [D, T], F32, kind="ExternalInput").ap()
    wq = nc.dram_tensor("wq", [NT, P, KT, P], F32, kind="ExternalInput").ap()
    wk = nc.dram_tensor("wk", [NT, P, KT, P], F32, kind="ExternalInput").ap()
    wv = nc.dram_tensor("wv", [KT, P, D], F32, kind="ExternalInput").ap()
    wo = nc.dram_tensor("wo", [NT, P, KT, P], F32, kind="ExternalInput").ap()
    bq = nc.dram_tensor("bq", [P, NT], F32, kind="ExternalInput").ap()
    bk = nc.dram_tensor("bk", [P, NT], F32, kind="ExternalInput").ap()
    bv = nc.dram_tensor("bv", [P, NT], F32, kind="ExternalInput").ap()
    bo = nc.dram_tensor("bo", [P, NT], F32, kind="ExternalInput").ap()
    wr = nc.dram_tensor("wr", [P, P], F32, kind="ExternalInput").ap()
    wg = nc.dram_tensor("wg", [E, NT, P, KT, P], F32, kind="ExternalInput").ap()
    wu = nc.dram_tensor("wu", [E, NT, P, KT, P], F32, kind="ExternalInput").ap()
    wd = nc.dram_tensor("wd", [E, NT, P, KT, P], F32, kind="ExternalInput").ap()
    identb = nc.dram_tensor("identb", [P, P], BF16, kind="ExternalInput").ap()
    identf = nc.dram_tensor("identf", [P, P], F32, kind="ExternalInput").ap()
    maskd = nc.dram_tensor("maskd", [P, P], F32, kind="ExternalInput").ap()
    outT = nc.dram_tensor("outT", [D, T], F32, kind="ExternalOutput").ap()
    qT_s = nc.dram_tensor("qT_s", [D, T], BF16, kind="Internal").ap()
    kT_s = nc.dram_tensor("kT_s", [D, T], BF16, kind="Internal").ap()
    vN_s = nc.dram_tensor("vN_s", [T, D], BF16, kind="Internal").ap()
    gat_s = nc.dram_tensor("gat_s", [8, T], BF16, kind="Internal").ap()
    ln2_s = nc.dram_tensor("ln2_s", [D, T], BF16, kind="Internal").ap()
    if debug:
        d_ln1 = nc.dram_tensor("d_ln1", [P, KT * T], BF16, kind="ExternalOutput").ap()
        d_q = nc.dram_tensor("d_q", [D, T], BF16, kind="ExternalOutput").ap()
        d_k = nc.dram_tensor("d_k", [D, T], BF16, kind="ExternalOutput").ap()
        d_v = nc.dram_tensor("d_v", [T, D], BF16, kind="ExternalOutput").ap()
        d_ln2 = nc.dram_tensor("d_ln2", [P, KT * T], BF16, kind="ExternalOutput").ap()
        d_gat = nc.dram_tensor("d_gat", [8, T], BF16, kind="ExternalOutput").ap()
        d_att = nc.dram_tensor("d_att", [P, H * T], BF16, kind="ExternalOutput").ap()

    st = ExitStack()
    sb = lambda name, shape, dt: st.enter_context(nc.sbuf_tensor(name, shape, dt))

    sPE = st.enter_context(nc.semaphore("sPE"))
    sV = st.enter_context(nc.semaphore("sV"))
    sA = st.enter_context(nc.semaphore("sA"))
    sG = st.enter_context(nc.semaphore("sG"))
    sS = st.enter_context(nc.semaphore("sS"))

    id_bf = sb("id_bf", [P, P], BF16)
    id_f = sb("id_f", [P, P], F32)
    mask_sb = sb("mask_sb", [P, P], F32)
    bq_sb = sb("bq_sb", [P, NT], F32)
    bk_sb = sb("bk_sb", [P, NT], F32)
    bv_sb = sb("bv_sb", [P, NT], F32)
    bo_sb = sb("bo_sb", [P, NT], F32)
    bqs_sb = sb("bqs_sb", [P, NT], F32)
    wr_bf = sb("wr_bf", [P, P], BF16)
    ones_c = sb("ones_c", [P, 1], BF16)
    ones_r = sb("ones_r", [1, P], BF16)
    wseg = [sb(f"wseg{i}", [P, KT * P], BF16) for i in range(2)]

    block = st.enter_context(nc.Block())

    tens = Q(nc.tensor, sPE)
    vec = Q(nc.vector, sV)
    act = Q(nc.scalar, sA)
    gps = Q(nc.gpsimd, sG)
    syn = Q(nc.sync, sS)
    ev = Ev()

    def vself(ins):
        vec.inc(ins)
        vec.wait(sV, vec.n)

    def gself(ins):
        gps.inc(ins)
        gps.wait(sG, gps.n)

    syn.inc(nc.sync.dma_start(id_bf[:], identb), 16)
    syn.inc(nc.sync.dma_start(id_f[:], identf), 16)
    syn.inc(nc.sync.dma_start(mask_sb[:], maskd), 16)
    syn.inc(nc.sync.dma_start(bq_sb[:], bq), 16)
    syn.inc(nc.sync.dma_start(bk_sb[:], bk), 16)
    syn.inc(nc.sync.dma_start(bv_sb[:], bv), 16)
    syn.inc(nc.sync.dma_start(bo_sb[:], bo), 16)
    ev.put("c:bias", syn)
    gps.inc(nc.gpsimd.dma_start(wr_bf[:], wr), 16)
    ev.put("c:wr", gps)
    vec.inc(nc.vector.memset(ones_c[:], 1.0))
    vec.inc(nc.vector.memset(ones_r[:], 1.0))
    ev.put("c:ones", vec)
    ev.wait(vec, "c:bias")
    vec.inc(nc.vector.tensor_scalar_mul(bqs_sb[:], bq_sb[:], ISQ))
    ev.put("c:bqs", vec)

    st1 = ExitStack()
    sb1 = lambda name, shape, dt: st1.enter_context(nc.sbuf_tensor(name, shape, dt))
    xt = sb1("xt", [P, KT * T], BF16)
    ln1 = sb1("ln1", [P, KT * T], BF16)
    tmp1 = [sb1(f"tmp1_{i}", [P, T], BF16) for i in range(2)]
    stage = [sb1(f"stage{i}", [P, T], BF16) for i in range(2)]
    rows = sb1("rows", [1, 3 * T], F32)
    rows_bf = sb1("rows_bf", [1, 2 * T], BF16)
    m_b = sb1("m_b", [P, T], F32)
    s_b = sb1("s_b", [P, T], F32)

    for k in range(KT):
        gps.inc(nc.gpsimd.dma_start(xt[:, k * T:(k + 1) * T], xT[k * P:(k + 1) * P, :]), 16)
        ev.put(f"p1:xt{k}", gps)
    for k in range(KT):
        ev.wait(gps, f"p1:xt{k}")
        gps.inc(nc.gpsimd.tensor_tensor(ln1[:, k * T:(k + 1) * T], xt[:, k * T:(k + 1) * T],
                                        xt[:, k * T:(k + 1) * T], ALU.mult))
        ev.put(f"p1:sq{k}", gps)

    stps = ExitStack()
    st_ps = [stps.enter_context(nc.psum_tensor(f"st_ps{j}", [P, 512], F32)) for j in range(8)]
    ev.wait(tens, "c:ones")
    ev.wait(tens, "p1:sq15")
    for c in range(CH):
        for s_i in range(2):
            src = xt if s_i == 0 else ln1
            for k in range(KT):
                ins = nc.tensor.matmul(st_ps[2 * c + s_i][0:1, :], ones_c[:],
                                       src[:, k * T + c * 512: k * T + (c + 1) * 512],
                                       start=(k == 0), stop=(k == KT - 1))
            tens.inc(ins)
            ev.put(f"p1:st{2 * c + s_i}", tens)
    for c in range(CH):
        ev.wait(vec, f"p1:st{2 * c}")
        vec.inc(nc.vector.tensor_copy(rows[0:1, c * 512:(c + 1) * 512], st_ps[2 * c][0:1, :]))
        ev.wait(vec, f"p1:st{2 * c + 1}")
        vec.inc(nc.vector.tensor_copy(rows[0:1, T + c * 512: T + (c + 1) * 512],
                                      st_ps[2 * c + 1][0:1, :]))
    ev.put("p1:stev", vec)
    vec.wait(sV, vec.n)
    r_m = rows[0:1, 0:T]
    r_v = rows[0:1, T:2 * T]
    r_t = rows[0:1, 2 * T:3 * T]
    vself(nc.vector.tensor_scalar_mul(r_m, r_m, 1.0 / D))
    vself(nc.vector.tensor_scalar_mul(r_v, r_v, 1.0 / D))
    vself(nc.vector.tensor_tensor(r_t, r_m, r_m, ALU.mult))
    vself(nc.vector.tensor_tensor(r_v, r_v, r_t, ALU.subtract))
    vself(nc.vector.tensor_scalar_add(r_v, r_v, EPS))
    vself(nc.vector.reciprocal(r_t, r_v))
    ev.put("p1:rowsv", vec)
    ev.wait(act, "p1:rowsv")
    ins = nc.scalar.activation(r_v, r_t, AF.Sqrt)
    act.inc(ins)
    ev.put("p1:srow", act)
    ev.wait(vec, "p1:srow")
    vself(nc.vector.tensor_copy(rows_bf[0:1, 0:T], r_m))
    vself(nc.vector.tensor_copy(rows_bf[0:1, T:2 * T], r_v))
    ev.put("p1:rowsbf", vec)
    stps.close()

    ps1 = ExitStack()
    ps_a = ps1.enter_context(nc.psum_tensor("ps_a", [P, T], F32))
    ps_b = ps1.enter_context(nc.psum_tensor("ps_b", [P, T], F32))
    ev.wait(tens, "p1:rowsbf")
    for c in range(CH):
        ins = nc.tensor.matmul(ps_a[:, c * 512:(c + 1) * 512], ones_r[:],
                               rows_bf[0:1, c * 512:(c + 1) * 512], start=True, stop=True)
        tens.inc(ins)
        ins = nc.tensor.matmul(ps_b[:, c * 512:(c + 1) * 512], ones_r[:],
                               rows_bf[0:1, T + c * 512: T + (c + 1) * 512], start=True, stop=True)
        tens.inc(ins)
    ev.put("p1:bcast", tens)
    ev.wait(vec, "p1:bcast")
    vec.inc(nc.vector.tensor_copy(m_b[:], ps_a[:]))
    vec.inc(nc.vector.tensor_copy(s_b[:], ps_b[:]))
    ev.put("p1:mbsb", vec)
    vec.wait(sV, vec.n)

    for k in range(KT):
        vself(nc.vector.tensor_tensor(tmp1[k % 2][:], xt[:, k * T:(k + 1) * T], m_b[:], ALU.subtract))
        vec.inc(nc.vector.tensor_tensor(ln1[:, k * T:(k + 1) * T], tmp1[k % 2][:], s_b[:], ALU.mult))
        ev.put(f"p1:ln1_{k}", vec)
    ev.put("p1:ln1", vec)

    def qk_pass(tag, wdram, dst, bias_ap, scale):
        for n in range(NT):
            if ev.has(f"{tag}:mm{n - 2}"):
                ev.wait(gps, f"{tag}:mm{n - 2}")
            elif tag == "k" and n < 2:
                ev.wait(gps, f"q:mm{NT - 2 + n}")
            gps.inc(nc.gpsimd.dma_start(wseg[n % 2][:], wdram[n].rearrange("p k j -> p (k j)")), 16)
            ev.put(f"{tag}:w{n}", gps)
            if tag == "k":
                ev.wait(gps, f"p1:ln1_{n}")
                gps.inc(nc.gpsimd.dma_start(xt[:, n * T:(n + 1) * T], wv[n]), 16)
                ev.put(f"v:w{n}", gps)
            pp = ps_a if n % 2 == 0 else ps_b
            ev.wait(tens, f"{tag}:w{n}")
            if ev.has(f"{tag}:ev{n - 2}"):
                ev.wait(tens, f"{tag}:ev{n - 2}")
            elif tag == "k" and n < 2:
                ev.wait(tens, f"q:ev{NT - 2 + n}")
            for k in range(KT):
                ev.wait(tens, f"p1:ln1_{k}")
                for c in range(CH):
                    ins = nc.tensor.matmul(pp[:, c * 512:(c + 1) * 512],
                                           wseg[n % 2][:, k * P:(k + 1) * P],
                                           ln1[:, k * T + c * 512: k * T + (c + 1) * 512],
                                           start=(k == 0), stop=(k == KT - 1))
            tens.inc(ins)
            ev.put(f"{tag}:mm{n}", tens)
            ev.wait(act, f"{tag}:mm{n}")
            if ev.has(f"{tag}:st{n - 2}"):
                ev.wait(act, f"{tag}:st{n - 2}")
            elif tag == "k" and n < 2:
                ev.wait(act, f"q:st{NT - 2 + n}")
            ins = nc.scalar.activation(stage[n % 2][:], pp[:], AF.Identity,
                                       bias=bias_ap[:, n:n + 1], scale=scale)
            act.inc(ins)
            ev.put(f"{tag}:ev{n}", act)
            ev.wait(syn, f"{tag}:ev{n}")
            syn.inc(nc.sync.dma_start(dst[n * P:(n + 1) * P, :], stage[n % 2][:]), 16)
            ev.put(f"{tag}:st{n}", syn)

    ev.wait(act, "c:bqs")
    qk_pass("q", wq, qT_s, bqs_sb, ISQ)
    qk_pass("k", wk, kT_s, bk_sb, 1.0)

    for t in range(NT):
        pp = ps_a if t % 2 == 0 else ps_b
        ev.wait(tens, f"v:w{KT - 1}")
        if ev.has(f"v:ev{t - 2}"):
            ev.wait(tens, f"v:ev{t - 2}")
        elif t < 2:
            ev.wait(tens, f"k:ev{NT - 2 + t}")
        for k in range(KT):
            for c in range(CH):
                ins = nc.tensor.matmul(pp[:, c * 512:(c + 1) * 512],
                                       ln1[:, k * T + t * P: k * T + (t + 1) * P],
                                       xt[:, k * T + c * 512: k * T + (c + 1) * 512],
                                       start=(k == 0), stop=(k == KT - 1))
        tens.inc(ins)
        ev.put(f"v:mm{t}", tens)
        ev.wait(act, f"v:mm{t}")
        if ev.has(f"v:st{t - 2}"):
            ev.wait(act, f"v:st{t - 2}")
        elif t < 2:
            ev.wait(act, f"k:st{NT - 2 + t}")
        ins = nc.scalar.activation(stage[t % 2][:], pp[:], AF.Identity)
        act.inc(ins)
        ev.put(f"v:ev{t}", act)
        ev.wait(syn, f"v:ev{t}")
        syn.inc(nc.sync.dma_start(vN_s[t * P:(t + 1) * P, :], stage[t % 2][:]), 16)
        ev.put(f"v:st{t}", syn)

    if debug:
        ev.wait(syn, "p1:ln1")
        syn.inc(nc.sync.dma_start(d_ln1, ln1[:]), 16)
        ev.wait(syn, f"q:st{NT - 1}")
        ev.wait(syn, f"k:st{NT - 1}")
        ev.wait(syn, f"v:st{NT - 1}")
        syn.inc(nc.sync.dma_start(d_q, qT_s), 16)
        syn.inc(nc.sync.dma_start(d_k, kT_s), 16)
        syn.inc(nc.sync.dma_start(d_v, vN_s), 16)

    p1_done = [f"v:ev{NT - 1}", f"v:st{NT - 1}", f"v:mm{NT - 1}", "p1:ln1",
               f"q:st{NT - 1}", f"k:st{NT - 1}", f"v:w{KT - 1}"]
    st1.close()
    ps1.close()

    st23 = ExitStack()
    sb23 = lambda name, shape, dt: st23.enter_context(nc.sbuf_tensor(name, shape, dt))


    st2 = ExitStack()
    sb2 = lambda name, shape, dt: st2.enter_context(nc.sbuf_tensor(name, shape, dt))
    qc = [sb2(f"qc{i}", [P, 512], BF16) for i in range(4)]
    kc = [sb2(f"kc{i}", [P, T], BF16) for i in range(4)]
    vc = [sb2(f"vc{i}", [P, KT * P], BF16) for i in range(4)]
    p_sb = [sb2(f"p_sb{i}", [P, T], BF16) for i in range(4)]
    p2 = [sb2(f"p2_{i}", [P, T], BF16) for i in range(4)]
    pT = sb2("pT", [P, KT * 512], BF16)
    attnT = sb2("attnT", [P, H * 512], BF16)
    x1c = sb2("x1c", [P, KT * 512], F32)
    xin = [sb2(f"xin{i}", [P, 512], F32) for i in range(2)]
    x1bf = [sb2(f"x1bf{i}", [P, 512], BF16) for i in range(2)]
    x2t = [sb2(f"x2t{i}", [P, 512], BF16) for i in range(2)]
    m2b = sb2("m2b", [P, 512], F32)
    s2b = sb2("s2b", [P, 512], F32)
    rows2 = sb2("rows2", [1, 3 * 512], F32)
    rows2bf = sb2("rows2bf", [1, 1024], BF16)
    nm = sb2("nm", [P, 24], F32)
    rs = sb2("rs", [P, 16], F32)
    ri = sb2("ri", [P, 8], F32)
    lgt = sb2("lgt", [8, 512], F32)
    lgn = sb2("lgn", [P, 32], F32)
    gw = sb2("gw", [P, 8], F32)
    gw2 = sb2("gw2", [P, 8], F32)
    gts = sb2("gts", [8, 512], BF16)
    ln2c = sb2("ln2c", [P, KT * 512], BF16)
    gm = sb2("gm", [P, 4], F32)
    gn = sb2("gn", [P, 32], F32)

    ps2 = ExitStack()
    ps_sc = ps2.enter_context(nc.psum_tensor("ps_sc", [P, 3072], F32))
    ps_tr = ps2.enter_context(nc.psum_tensor("ps_tr", [P, 1024], BF16))
    ps_av = ps2.enter_context(nc.psum_tensor("ps_av", [P, 512], F32))

    for qq in (tens, vec, act, gps, syn):
        for e in p1_done:
            ev.wait(qq, e)

    last_exp = [None, None]
    last_trev = [None, None]
    last_p_user = [None] * 4
    last_p2tr = [None] * 4
    last_pexp = [None] * 3
    pcn = [0]
    pending_stores = []

    def flush_stores():
        while pending_stores:
            kind, cc, n = pending_stores.pop(0)
            if kind == "ln2":
                ev.wait(syn, f"p2:ln2:{cc}:{n}")
                syn.inc(nc.sync.dma_start(ln2_s[n * P:(n + 1) * P, cc * 512:(cc + 1) * 512],
                                          ln2c[:, n * 512:(n + 1) * 512]), 16)
                ev.put(f"p2:ln2st:{cc}:{n}", syn)
            else:
                ev.wait(syn, f"p2:gates:{cc}")
                syn.inc(nc.sync.dma_start(gat_s[0:8, cc * 512:(cc + 1) * 512], gts[:]), 16)
                ev.put(f"p2:gatst:{cc}", syn)

    def emit_router(cc, stage):
        if stage == 0:
            s = pcn[0] % 3
            pcn[0] += 1
            off = s * 1024
            ev.wait(tens, f"p2:ln2:{cc}")
            ev.wait(tens, "c:wr")
            if last_pexp[s] is not None:
                ev.wait(tens, last_pexp[s])
            for k in range(KT):
                ins = nc.tensor.matmul(ps_sc[0:8, off:off + 512], wr_bf[:, k * 8:(k + 1) * 8],
                                       ln2c[:, k * 512:(k + 1) * 512],
                                       start=(k == 0), stop=(k == KT - 1))
            tens.inc(ins)
            ev.put(f"p2:lg:{cc}", tens)
            ev.wait(vec, f"p2:lg:{cc}")
            vself(nc.vector.tensor_copy(lgt[:], ps_sc[0:8, off:off + 512]))
            ev.put(f"p2:lgt:{cc}", vec)
            last_pexp[s] = f"p2:lgt:{cc}"
            return
        if stage == 2:
            emit_router_gates(cc)
            return
        s2_ = pcn[0] % 3
        pcn[0] += 1
        off2 = s2_ * 1024
        ev.wait(tens, f"p2:lgt:{cc}")
        if last_pexp[s2_] is not None:
            ev.wait(tens, last_pexp[s2_])
        for u in range(4):
            ins = nc.tensor.transpose(ps_sc[:, off2 + u * 8: off2 + (u + 1) * 8],
                                      lgt[0:8, u * P:(u + 1) * P], id_f[0:8, 0:8])
        tens.inc(ins)
        ev.put(f"p2:lgtr:{cc}", tens)
        ev.wait(vec, f"p2:lgtr:{cc}")
        vself(nc.vector.tensor_copy(lgn[:], ps_sc[:, off2:off2 + 32]))
        ev.put(f"p2:lgn:{cc}", vec)
        last_pexp[s2_] = f"p2:lgn:{cc}"
        for u in range(4):
            lu = lgn[:, u * 8:(u + 1) * 8]
            vself(nc.vector.reduce_max(gm[:, 0:1], lu, axis=AX))
            vself(nc.vector.tensor_single_scalar(gw[:], lu, gm[:, 0:1], ALU.is_equal))
            vself(nc.vector.scalar_tensor_tensor(gw2[:], gw[:], 1.0e30, lu, ALU.mult, ALU.subtract))
            vself(nc.vector.tensor_reduce(gm[:, 1:2], gw2[:], axis=AX, op=ALU.min, negate=True))
            vself(nc.vector.tensor_tensor(gm[:, 2:3], gm[:, 0:1], gm[:, 1:2], ALU.subtract))
            ev.put(f"p2:gd:{cc}:{u}", vec)
            ev.wait(act, f"p2:gd:{cc}:{u}")
            act.inc(nc.scalar.activation(gm[:, 2:3], gm[:, 2:3], AF.Sigmoid))
            ev.put(f"p2:gp:{cc}:{u}", act)
            ev.wait(vec, f"p2:gp:{cc}:{u}")
            vself(nc.vector.tensor_scalar(gm[:, 3:4], gm[:, 2:3], -1.0, 1.0, ALU.mult, ALU.add))
            vself(nc.vector.tensor_single_scalar(gw2[:], lu, gm[:, 1:2], ALU.is_equal))
            vself(nc.vector.tensor_scalar_mul(gw[:], gw[:], gm[:, 2:3]))
            vself(nc.vector.tensor_scalar_mul(gw2[:], gw2[:], gm[:, 3:4]))
            vself(nc.vector.tensor_tensor(gn[:, u * 8:(u + 1) * 8], gw[:], gw2[:], ALU.add))
        ev.put(f"p2:gn:{cc}", vec)

    def emit_router_gates(cc):
        s3_ = pcn[0] % 3
        pcn[0] += 1
        off3 = s3_ * 1024
        ev.wait(tens, f"p2:gn:{cc}")
        if last_pexp[s3_] is not None:
            ev.wait(tens, last_pexp[s3_])
        for u in range(4):
            ins = nc.tensor.transpose(ps_sc[0:8, off3 + u * P: off3 + (u + 1) * P],
                                      gn[:, u * 8:(u + 1) * 8], id_f[:])
        tens.inc(ins)
        ev.put(f"p2:gtr:{cc}", tens)
        ev.wait(vec, f"p2:gtr:{cc}")
        if ev.has(f"p2:gatst:{cc - 1}"):
            ev.wait(vec, f"p2:gatst:{cc - 1}")
        vec.inc(nc.vector.tensor_copy(gts[:], ps_sc[0:8, off3:off3 + 512]))
        ev.put(f"p2:gates:{cc}", vec)
        last_pexp[s3_] = f"p2:gates:{cc}"
        pending_stores.append(("gat", cc, 0))
    for c in range(CH):
        JT = 4 * c + 4
        L_MAX = JT * P
        def emit_loads(h):
            if ev.has(f"p2:sc:{c}:{h - 4}:3"):
                ev.wait(syn, f"p2:sc:{c}:{h - 4}:3")
            elif c > 0 and h < 4:
                ev.wait(syn, f"p2:sc:{c - 1}:{H - 4 + h}:3")
            else:
                ev.wait(syn, f"q:st{NT - 1}")
            syn.inc(nc.sync.dma_start(qc[h % 4][:], qT_s[h * P:(h + 1) * P, c * 512:(c + 1) * 512]), 16)
            syn.inc(nc.sync.dma_start(kc[h % 4][:, 0:L_MAX], kT_s[h * P:(h + 1) * P, 0:L_MAX]), 16)
            ev.put(f"p2:kc:{c}:{h}", syn)
            if ev.has(f"p2:av:{c}:{h - 4}"):
                ev.wait(act, f"p2:av:{c}:{h - 4}")
            elif c > 0 and h < 4:
                ev.wait(act, f"p2:av:{c - 1}:{H - 4 + h}")
            act.inc(nc.scalar.dma_start(
                vc[h % 4][:, 0:L_MAX].rearrange("p (j d) -> p j d", d=P),
                vN_s.rearrange("(j p) d -> p j d", p=P)[:, 0:JT, h * P:(h + 1) * P]), 16)
            ev.put(f"p2:vc:{c}:{h}", act)

        def emit_scores(h, bs):
            for b in bs:
                tg = 4 * c + b
                L = (tg + 1) * P
                L0 = min(L, 1024)
                two = L > 1024
                slot = (h % 2) * 2 + (b % 2)
                cA = (h % 2) * 4 + b
                s0 = pcn[0] % 3
                pcn[0] += 1
                o0 = s0 * 1024
                ev.wait(tens, f"p2:kc:{c}:{h}")
                if last_pexp[s0] is not None:
                    ev.wait(tens, last_pexp[s0])
                for j0 in range(0, L0, 512):
                    wdt = min(512, L0 - j0)
                    ins = nc.tensor.matmul(ps_sc[:, o0 + j0: o0 + j0 + wdt],
                                           qc[h % 4][:, b * P:(b + 1) * P],
                                           kc[h % 4][:, j0: j0 + wdt],
                                           start=True, stop=True)
                tens.inc(ins)
                ev.put(f"p2:sc0:{c}:{h}:{b}", tens)
                if two:
                    s1 = pcn[0] % 3
                    pcn[0] += 1
                    o1 = s1 * 1024
                    if last_pexp[s1] is not None:
                        ev.wait(tens, last_pexp[s1])
                    for j0 in range(1024, L, 512):
                        wdt = min(512, L - j0)
                        ins = nc.tensor.matmul(ps_sc[:, o1 + j0 - 1024: o1 + j0 - 1024 + wdt],
                                               qc[h % 4][:, b * P:(b + 1) * P],
                                               kc[h % 4][:, j0: j0 + wdt],
                                               start=True, stop=True)
                    tens.inc(ins)
                ev.put(f"p2:sc:{c}:{h}:{b}", tens)
                if ev.has(f"p2:exp:{c}:{h - 2}:{b}"):
                    ev.wait(vec, f"p2:exp:{c}:{h - 2}:{b}")
                elif c > 0 and h < 2 and ev.has(f"p2:exp:{c - 1}:{H - 2 + h}:{b}"):
                    ev.wait(vec, f"p2:exp:{c - 1}:{H - 2 + h}:{b}")
                if not two:
                    ev.wait(vec, f"p2:sc0:{c}:{h}:{b}")
                    vec.inc(nc.vector.tensor_tensor(ps_sc[:, o0 + tg * P: o0 + L],
                                                    ps_sc[:, o0 + tg * P: o0 + L],
                                                    mask_sb[:], ALU.add))
                    vec.wait(sV, vec.n)
                    vec.inc(nc.vector.reduce_max(nm[:, cA:cA + 1], ps_sc[:, o0: o0 + L],
                                                 axis=AX, negate=True))
                    ev.put(f"p2:max:{c}:{h}:{b}", vec)
                else:
                    ev.wait(vec, f"p2:sc0:{c}:{h}:{b}")
                    vec.inc(nc.vector.reduce_max(nm[:, 8 + cA:9 + cA], ps_sc[:, o0: o0 + 1024],
                                                 axis=AX, negate=True))
                    ev.wait(vec, f"p2:sc:{c}:{h}:{b}")
                    vec.inc(nc.vector.tensor_tensor(ps_sc[:, o1 + tg * P - 1024: o1 + L - 1024],
                                                    ps_sc[:, o1 + tg * P - 1024: o1 + L - 1024],
                                                    mask_sb[:], ALU.add))
                    vec.wait(sV, vec.n)
                    vself(nc.vector.reduce_max(nm[:, 16 + cA:17 + cA],
                                               ps_sc[:, o1: o1 + L - 1024],
                                               axis=AX, negate=True))
                    vec.inc(nc.vector.tensor_tensor(nm[:, cA:cA + 1], nm[:, 8 + cA:9 + cA],
                                                    nm[:, 16 + cA:17 + cA], ALU.min))
                    ev.put(f"p2:max:{c}:{h}:{b}", vec)
                ev.wait(act, f"p2:max:{c}:{h}:{b}")
                if last_p_user[slot] is not None:
                    ev.wait(act, last_p_user[slot])
                if ev.has(f"p2:ri:{c}:{h - 2}:{b}"):
                    ev.wait(act, f"p2:ri:{c}:{h - 2}:{b}")
                ins = nc.scalar.activation(p_sb[slot][:, 0:L0], ps_sc[:, o0: o0 + L0], AF.Exp,
                                           bias=nm[:, cA:cA + 1], scale=1.0,
                                           accum_out=rs[:, cA:cA + 1])
                act.inc(ins)
                ev.put(f"p2:exp0:{c}:{h}:{b}", act)
                last_pexp[s0] = f"p2:exp0:{c}:{h}:{b}"
                if two:
                    ins = nc.scalar.activation(p_sb[slot][:, 1024:L], ps_sc[:, o1: o1 + L - 1024],
                                               AF.Exp, bias=nm[:, cA:cA + 1], scale=1.0,
                                               accum_out=rs[:, 8 + cA:9 + cA])
                    act.inc(ins)
                    ev.put(f"p2:exp:{c}:{h}:{b}", act)
                    last_pexp[s1] = f"p2:exp:{c}:{h}:{b}"
                else:
                    ev.put(f"p2:exp:{c}:{h}:{b}", act)

        def emit_ptz(h):
            for b in range(4):
                tg = 4 * c + b
                if tg + 1 < JT:
                    zap = pT[:].rearrange("p (j u) -> p j u", u=512)[:, tg + 1:JT, b * P:(b + 1) * P]
                    if ev.has(f"p2:avdone:{c}:{h - 1}"):
                        ev.wait(vec, f"p2:avdone:{c}:{h - 1}")
                    elif c > 0 and h == 0:
                        ev.wait(vec, f"p2:avdone:{c - 1}:{H - 1}")
                    vec.inc(nc.vector.memset(zap, 0.0))
                    ev.put(f"p2:ptz:{c}:{h}:{b}", vec)

        def emit_tr(h, bs):
            for b in bs:
                tg = 4 * c + b
                L = (tg + 1) * P
                slot = (h % 2) * 2 + (b % 2)
                cA = (h % 2) * 4 + b
                two = L > 1024
                ev.wait(vec, f"p2:exp:{c}:{h}:{b}")
                if two:
                    vself(nc.vector.tensor_tensor(rs[:, cA:cA + 1], rs[:, cA:cA + 1],
                                                  rs[:, 8 + cA:9 + cA], ALU.add))
                vself(nc.vector.reciprocal(ri[:, cA:cA + 1], rs[:, cA:cA + 1]))
                ev.put(f"p2:ri:{c}:{h}:{b}", vec)
                if last_p2tr[slot] is not None:
                    ev.wait(vec, last_p2tr[slot])
                vec.inc(nc.vector.tensor_scalar_mul(p2[slot][:, 0:L], p_sb[slot][:, 0:L],
                                                    ri[:, cA:cA + 1]))
                ev.put(f"p2:pmul:{c}:{h}:{b}", vec)
                last_p_user[slot] = f"p2:pmul:{c}:{h}:{b}"
                ntile = tg + 1
                rr = 0
                for r0 in range(0, ntile, 8):
                    slots = min(8, ntile - r0)
                    ev.wait(tens, f"p2:pmul:{c}:{h}:{b}")
                    if last_trev[0] is not None:
                        ev.wait(tens, last_trev[0])
                    for j in range(r0, r0 + slots):
                        ins = nc.tensor.transpose(
                            ps_tr[:, (j - r0) * P: (j - r0 + 1) * P],
                            p2[slot][:, j * P:(j + 1) * P], id_bf[:])
                    tens.inc(ins)
                    ev.put(f"p2:tr:{c}:{h}:{b}:{rr}", tens)
                    eng = vec if b % 2 == 0 else act
                    ev.wait(eng, f"p2:tr:{c}:{h}:{b}:{rr}")
                    if ev.has(f"p2:avdone:{c}:{h - 1}"):
                        ev.wait(eng, f"p2:avdone:{c}:{h - 1}")
                    elif c > 0 and h == 0:
                        ev.wait(eng, f"p2:avdone:{c - 1}:{H - 1}")
                    dst = pT[:].rearrange("p (j u) -> p j u", u=512)[:, r0:r0 + slots, b * P:(b + 1) * P]
                    srcv = ps_tr[:, 0: slots * P].rearrange("p (j d) -> p j d", d=P)
                    if eng is vec:
                        eng.inc(nc.vector.tensor_copy(dst, srcv))
                    else:
                        eng.inc(nc.scalar.copy(dst, srcv))
                    ev.put(f"p2:trev:{c}:{h}:{b}:{rr}", eng)
                    last_trev[0] = f"p2:trev:{c}:{h}:{b}:{rr}"
                    rr += 1
                last_p2tr[slot] = f"p2:tr:{c}:{h}:{b}:{rr - 1}"
                ev.put(f"p2:trevl:{c}:{h}:{b}", eng)

        def emit_av(h):
            for b in range(4):
                ev.wait(tens, f"p2:trevl:{c}:{h}:{b}")
            for b in range(3):
                if ev.has(f"p2:ptz:{c}:{h}:{b}"):
                    ev.wait(tens, f"p2:ptz:{c}:{h}:{b}")
            ev.wait(tens, f"p2:vc:{c}:{h}")
            if ev.has(f"p2:av:{c}:{h - 1}"):
                ev.wait(tens, f"p2:av:{c}:{h - 1}")
            elif c > 0 and h == 0:
                ev.wait(tens, f"p2:av:{c - 1}:{NT - 1}")
                ev.wait(tens, f"p2:m2b:{c - 1}")
            for j in range(JT):
                ins = nc.tensor.matmul(ps_av[:, 0:512],
                                       vc[h % 4][:, j * P:(j + 1) * P],
                                       pT[:, j * 512:(j + 1) * 512],
                                       start=(j == 0), stop=(j == JT - 1))
            tens.inc(ins)
            ev.put(f"p2:avmm:{c}:{h}", tens)
            ev.put(f"p2:avdone:{c}:{h}", tens)
            ev.wait(act, f"p2:avmm:{c}:{h}")
            if c > 0 and h == 0:
                ev.wait(act, f"p2:op:{c - 1}:{NT - 1}")
            ins = nc.scalar.activation(attnT[:, h * 512:(h + 1) * 512],
                                       ps_av[:, 0:512],
                                       AF.Identity, bias=bv_sb[:, h:h + 1])
            act.inc(ins)
            ev.put(f"p2:av:{c}:{h}", act)

        for n in range(2):
            if c > 0:
                ev.wait(gps, f"p2:op:{c - 1}:{NT - 2 + n}")
            gps.inc(nc.gpsimd.dma_start(wseg[n % 2][:], wo[n].rearrange("p k j -> p (k j)")), 16)
            ev.put(f"p2:wo:{c}:{n}", gps)

        for hh in range(H + 2):
            if hh < H:
                emit_loads(hh)
            if hh == 6:
                flush_stores()
            if c > 0 and hh in (7, 9, 11):
                emit_router(c - 1, {7: 0, 9: 1, 11: 2}[hh])
            if hh >= 2:
                emit_ptz(hh - 2)
            if hh >= 2:
                emit_tr(hh - 2, [0, 1, 2, 3])
            if hh < H:
                emit_scores(hh, [0, 1, 2, 3])
            if hh >= 2:
                emit_av(hh - 2)

        if debug:
            ev.wait(syn, f"p2:av:{c}:{H - 1}")
            syn.inc(nc.sync.dma_start(
                d_att.rearrange("p (h z u) -> p h z u", z=CH, u=512)[:, :, c, :],
                attnT[:].rearrange("p (h u) -> p h u", u=512)), 16)

        for n in range(NT):
            if n >= 2:
                ev.wait(gps, f"p2:op:{c}:{n - 2}")
                gps.inc(nc.gpsimd.dma_start(wseg[n % 2][:], wo[n].rearrange("p k j -> p (k j)")), 16)
                ev.put(f"p2:wo:{c}:{n}", gps)
            if ev.has(f"p2:x1:{c}:{n - 2}"):
                ev.wait(syn, f"p2:x1:{c}:{n - 2}")
            elif c > 0 and n < 2:
                ev.wait(syn, f"p2:x1:{c - 1}:{NT - 2 + n}")
            syn.inc(nc.sync.dma_start(xin[n % 2][:], xT[n * P:(n + 1) * P, c * 512:(c + 1) * 512]), 16)
            ev.put(f"p2:xin:{c}:{n}", syn)
            ev.wait(tens, f"p2:wo:{c}:{n}")
            ev.wait(tens, f"p2:av:{c}:{H - 1}")
            if ev.has(f"p2:x1:{c}:{n - 1}"):
                ev.wait(tens, f"p2:x1:{c}:{n - 1}")
            for k in range(KT):
                ins = nc.tensor.matmul(ps_av[:, 0:512],
                                       wseg[n % 2][:, k * P:(k + 1) * P],
                                       attnT[:, k * 512:(k + 1) * 512],
                                       start=(k == 0), stop=(k == KT - 1))
            tens.inc(ins)
            ev.put(f"p2:op:{c}:{n}", tens)
            ev.wait(vec, f"p2:op:{c}:{n}")
            ev.wait(vec, f"p2:xin:{c}:{n}")
            vec.inc(nc.vector.scalar_tensor_tensor(
                x1c[:, n * 512:(n + 1) * 512], xin[n % 2][:], bo_sb[:, n:n + 1],
                ps_av[:, 0:512], ALU.add, ALU.add))
            ev.put(f"p2:x1:{c}:{n}", vec)
            ev.wait(gps, f"p2:x1:{c}:{n}")
            gps.inc(nc.gpsimd.dma_start(outT[n * P:(n + 1) * P, c * 512:(c + 1) * 512],
                                        x1c[:, n * 512:(n + 1) * 512]), 16)
            ev.put(f"p2:x1st:{c}:{n}", gps)

        for n in range(NT):
            if ev.has(f"p2:stpe:{c}:{n - 2}"):
                ev.wait(vec, f"p2:stpe:{c}:{n - 2}")
            vself(nc.vector.tensor_copy(x1bf[n % 2][:], x1c[:, n * 512:(n + 1) * 512]))
            vec.inc(nc.vector.tensor_tensor(x2t[n % 2][:], x1bf[n % 2][:], x1bf[n % 2][:], ALU.mult))
            ev.put(f"p2:x2t:{c}:{n}", vec)
            ev.wait(tens, f"p2:x2t:{c}:{n}")
            if n == 0:
                ev.wait(tens, f"p2:x1:{c}:{NT - 1}")
            ins = nc.tensor.matmul(ps_av[0:1, 0:512], ones_c[:], x1bf[n % 2][:],
                                   start=(n == 0), stop=(n == NT - 1))
            ins2 = nc.tensor.matmul(ps_sc[0:1, 0:512], ones_c[:], x2t[n % 2][:],
                                    start=(n == 0), stop=(n == NT - 1))
            tens.inc(ins2)
            ev.put(f"p2:stpe:{c}:{n}", tens)
        ev.put(f"p2:st2:{c}", tens)

        ev.wait(vec, f"p2:st2:{c}")
        vself(nc.vector.tensor_copy(rows2[0:1, 0:512], ps_av[0:1, 0:512]))
        vself(nc.vector.tensor_copy(rows2[0:1, 512:1024], ps_sc[0:1, 0:512]))
        r2m = rows2[0:1, 0:512]
        r2v = rows2[0:1, 512:1024]
        r2t = rows2[0:1, 1024:1536]
        vself(nc.vector.tensor_scalar_mul(r2m, r2m, 1.0 / D))
        vself(nc.vector.tensor_scalar_mul(r2v, r2v, 1.0 / D))
        vself(nc.vector.tensor_tensor(r2t, r2m, r2m, ALU.mult))
        vself(nc.vector.tensor_tensor(r2v, r2v, r2t, ALU.subtract))
        vself(nc.vector.tensor_scalar_add(r2v, r2v, EPS))
        vself(nc.vector.reciprocal(r2t, r2v))
        ev.put(f"p2:r2v:{c}", vec)
        ev.wait(act, f"p2:r2v:{c}")
        act.inc(nc.scalar.activation(r2v, r2t, AF.Sqrt))
        ev.put(f"p2:r2s:{c}", act)
        ev.wait(vec, f"p2:r2s:{c}")
        vself(nc.vector.tensor_copy(rows2bf[0:1, 0:512], r2m))
        vself(nc.vector.tensor_copy(rows2bf[0:1, 512:1024], r2v))
        ev.put(f"p2:r2bf:{c}", vec)
        ev.wait(tens, f"p2:r2bf:{c}")
        ins = nc.tensor.matmul(ps_av[:, 0:512], ones_r[:], rows2bf[0:1, 0:512],
                               start=True, stop=True)
        tens.inc(ins)
        ins = nc.tensor.matmul(ps_sc[:, 0:512], ones_r[:], rows2bf[0:1, 512:1024],
                               start=True, stop=True)
        tens.inc(ins)
        ev.put(f"p2:m2bc:{c}", tens)
        ev.wait(vec, f"p2:m2bc:{c}")
        vec.inc(nc.vector.tensor_copy(m2b[:], ps_av[:, 0:512]))
        vec.inc(nc.vector.tensor_copy(s2b[:], ps_sc[:, 0:512]))
        ev.put(f"p2:m2b:{c}", vec)
        vec.wait(sV, vec.n)
        for n in range(NT):
            vself(nc.vector.tensor_tensor(x1bf[n % 2][:], x1c[:, n * 512:(n + 1) * 512],
                                          m2b[:], ALU.subtract))
            if ev.has(f"p2:ln2st:{c - 1}:{n}"):
                ev.wait(vec, f"p2:ln2st:{c - 1}:{n}")
            if ev.has(f"p2:lg:{c - 1}"):
                ev.wait(vec, f"p2:lg:{c - 1}")
            vec.inc(nc.vector.tensor_tensor(ln2c[:, n * 512:(n + 1) * 512],
                                            x1bf[n % 2][:], s2b[:], ALU.mult))
            ev.put(f"p2:ln2:{c}:{n}", vec)
            pending_stores.append(("ln2", c, n))
        ev.put(f"p2:ln2:{c}", vec)

        last_pexp[0] = f"p2:m2b:{c}"

    for stg in range(3):
        emit_router(CH - 1, stg)
    flush_stores()

    if debug:
        for n in range(NT):
            ev.wait(syn, f"p2:ln2st:{CH - 1}:{n}")
        syn.inc(nc.sync.dma_start(
            d_ln2.rearrange("p (k t) -> p k t", t=T),
            ln2_s.rearrange("(k p) t -> p k t", p=P)), 16)
        ev.wait(syn, f"p2:gatst:{CH - 1}")
        syn.inc(nc.sync.dma_start(d_gat, gat_s), 16)

    p2_done = [f"p2:gatst:{CH - 1}", f"p2:ln2st:{CH - 1}:{NT - 1}", f"p2:ln2:{CH - 1}", f"p2:gtr:{CH - 1}",
               f"p2:x1st:{CH - 1}:{NT - 1}", f"p2:wo:{CH - 1}:{NT - 1}",
               f"p2:vc:{CH - 1}:{H - 1}"]
    st2.close()
    ps2.close()

    st3 = ExitStack()
    sb3 = lambda name, shape, dt: st3.enter_context(nc.sbuf_tensor(name, shape, dt))
    ln2f = sb3("ln2f", [P, KT * T], BF16)
    u_sb = sb3("u_sb", [P, KT * T], BF16)
    growall = sb3("growall", [1, E * T], BF16)
    gate_b = sb3("gate_b", [P, T], F32)
    h_sb = [sb3(f"h_sb{i}", [P, T], F32) for i in range(2)]
    gtmp = [sb3(f"gtmp{i}", [P, T], BF16) for i in range(2)]
    ps3 = ExitStack()
    ps3a = ps3.enter_context(nc.psum_tensor("ps3a", [P, T], F32))
    ps3b = ps3.enter_context(nc.psum_tensor("ps3b", [P, T], F32))

    for qq in (tens, vec, act, syn):
        for e in p2_done:
            ev.wait(qq, e)

    for k in range(KT):
        if k % 2 == 0:
            syn.inc(nc.sync.dma_start(ln2f[:, k * T:(k + 1) * T], ln2_s[k * P:(k + 1) * P, :]), 16)
            ev.put(f"p3:ln2f{k}", syn)
        else:
            act.inc(nc.scalar.dma_start(ln2f[:, k * T:(k + 1) * T], ln2_s[k * P:(k + 1) * P, :]), 16)
            ev.put(f"p3:ln2f{k}", act)

    syn.inc(nc.sync.dma_start(growall[:], gat_s.rearrange("e t -> (e t)").rearrange("(o f) -> o f", o=1)), 16)
    ev.put("p3:growall", syn)

    def emit_gb(e):
        ev.wait(tens, "p3:growall")
        if e > 0:
            ev.wait(tens, f"p3:h:{e - 1}:4")
        for c in range(CH):
            ins = nc.tensor.matmul(ps3a[:, c * 512:(c + 1) * 512], ones_r[:],
                                   growall[0:1, e * T + c * 512: e * T + (c + 1) * 512],
                                   start=True, stop=True)
        tens.inc(ins)
        ev.put(f"p3:gb:{e}", tens)
        ev.wait(act, f"p3:gb:{e}")
        if e > 0:
            ev.wait(act, f"p3:u:{e - 1}:{NT - 1}")
        act.inc(nc.scalar.copy(gate_b[:], ps3a[:]))
        ev.put(f"p3:gbe:{e}", act)

    def emit_acc(e, n):
        ev.wait(gps, f"p3:h:{e}:{n}")
        if ev.has(f"p3:acc:{e - 1}:{n}"):
            ev.wait(gps, f"p3:acc:{e - 1}:{n}")
        else:
            ev.wait(gps, f"p2:x1st:3:{n}")
        gps.inc(nc.gpsimd.dma_start(outT[n * P:(n + 1) * P, :], h_sb[n % 2][:],
                                    accum_op=ALU.add), 16)
        ev.put(f"p3:acc:{e}:{n}", gps)

    emit_gb(0)
    for e in range(E):
        for n in range(NT):
            wkey = f"p3:uw:{e}:{n}"
            if ev.has(f"p3:umm:{e}:{n - 2}"):
                ev.wait(gps, f"p3:umm:{e}:{n - 2}")
            elif e > 0 and n < 2:
                ev.wait(gps, f"p3:dmm:{e - 1}:{NT - 2 + n}")
            else:
                ev.wait(gps, f"p2:op:{CH - 1}:{NT - 1}")
            gps.inc(nc.gpsimd.dma_start(wseg[n % 2][:], wu[e, n].rearrange("p k j -> p (k j)")), 16)
            ev.put(wkey, gps)
            if e > 0 and n < 2:
                emit_acc(e - 1, NT - 2 + n)
            pp = ps3a if n % 2 == 0 else ps3b
            ev.wait(tens, wkey)
            ev.wait(tens, f"p3:gbe:{e}")
            if ev.has(f"p3:u:{e}:{n - 2}"):
                ev.wait(tens, f"p3:u:{e}:{n - 2}")
            elif e > 0 and n < 2:
                ev.wait(tens, f"p3:h:{e - 1}:{NT - 2 + n}")
            for k in range(KT):
                ev.wait(tens, f"p3:ln2f{k}")
                for c in range(CH):
                    ins = nc.tensor.matmul(pp[:, c * 512:(c + 1) * 512],
                                           wseg[n % 2][:, k * P:(k + 1) * P],
                                           ln2f[:, k * T + c * 512: k * T + (c + 1) * 512],
                                           start=(k == 0), stop=(k == KT - 1))
            tens.inc(ins)
            ev.put(f"p3:umm:{e}:{n}", tens)
            ev.wait(vec, f"p3:umm:{e}:{n}")
            ev.wait(vec, f"p3:gbe:{e}")
            for c in range(CH):
                vec.inc(nc.vector.scalar_tensor_tensor(
                    u_sb[:, n * T + c * 512: n * T + (c + 1) * 512],
                    gate_b[:, c * 512:(c + 1) * 512], 1.0,
                    pp[:, c * 512:(c + 1) * 512], ALU.mult, ALU.mult))
            ev.put(f"p3:u:{e}:{n}", vec)

        for n in range(NT):
            wkey = f"p3:gw:{e}:{n}"
            if ev.has(f"p3:gmm:{e}:{n - 2}"):
                ev.wait(gps, f"p3:gmm:{e}:{n - 2}")
            else:
                ev.wait(gps, f"p3:umm:{e}:{NT - 2 + n}")
            gps.inc(nc.gpsimd.dma_start(wseg[n % 2][:], wg[e, n].rearrange("p k j -> p (k j)")), 16)
            ev.put(wkey, gps)
            pp = ps3a if n % 2 == 0 else ps3b
            ev.wait(tens, wkey)
            if ev.has(f"p3:sil:{e}:{n - 2}"):
                ev.wait(tens, f"p3:sil:{e}:{n - 2}")
            else:
                ev.wait(tens, f"p3:u:{e}:{NT - 2 + n}")
            for k in range(KT):
                for c in range(CH):
                    ins = nc.tensor.matmul(pp[:, c * 512:(c + 1) * 512],
                                           wseg[n % 2][:, k * P:(k + 1) * P],
                                           ln2f[:, k * T + c * 512: k * T + (c + 1) * 512],
                                           start=(k == 0), stop=(k == KT - 1))
            tens.inc(ins)
            ev.put(f"p3:gmm:{e}:{n}", tens)
            ev.wait(act, f"p3:gmm:{e}:{n}")
            if ev.has(f"p3:prod:{e}:{n - 2}"):
                ev.wait(act, f"p3:prod:{e}:{n - 2}")
            ins = nc.scalar.activation(gtmp[n % 2][:], pp[:], AF.Silu)
            act.inc(ins)
            ev.put(f"p3:sil:{e}:{n}", act)
            ev.wait(vec, f"p3:sil:{e}:{n}")
            vself(nc.vector.tensor_scalar_min(gtmp[n % 2][:], gtmp[n % 2][:], 7.0))
            ev.put(f"p3:clip:{e}:{n}", vec)
            ev.wait(vec, f"p3:u:{e}:{n}")
            vec.inc(nc.vector.tensor_tensor(u_sb[:, n * T:(n + 1) * T], gtmp[n % 2][:],
                                            u_sb[:, n * T:(n + 1) * T], ALU.mult))
            ev.put(f"p3:prod:{e}:{n}", vec)
        ev.put(f"p3:prod:{e}", vec)

        for n in range(NT):
            wkey = f"p3:dw:{e}:{n}"
            if ev.has(f"p3:dmm:{e}:{n - 2}"):
                ev.wait(gps, f"p3:dmm:{e}:{n - 2}")
            else:
                ev.wait(gps, f"p3:gmm:{e}:{NT - 2 + n}")
            gps.inc(nc.gpsimd.dma_start(wseg[n % 2][:], wd[e, n].rearrange("p k j -> p (k j)")), 16)
            ev.put(wkey, gps)
            if n == 6 and e + 1 < E:
                emit_gb(e + 1)
            pp = ps3a if n % 2 == 0 else ps3b
            ev.wait(tens, wkey)
            ev.wait(tens, f"p3:prod:{e}")
            if n == 6 and e + 1 < E:
                ev.wait(tens, f"p3:gbe:{e + 1}")
            if ev.has(f"p3:h:{e}:{n - 2}"):
                ev.wait(tens, f"p3:h:{e}:{n - 2}")
            else:
                ev.wait(tens, f"p3:sil:{e}:{NT - 2 + n}")
            for k in range(KT):
                for c in range(CH):
                    ins = nc.tensor.matmul(pp[:, c * 512:(c + 1) * 512],
                                           wseg[n % 2][:, k * P:(k + 1) * P],
                                           u_sb[:, k * T + c * 512: k * T + (c + 1) * 512],
                                           start=(k == 0), stop=(k == KT - 1))
            tens.inc(ins)
            ev.put(f"p3:dmm:{e}:{n}", tens)
            if n >= 2:
                emit_acc(e, n - 2)
            ev.wait(vec, f"p3:dmm:{e}:{n}")
            if ev.has(f"p3:acc:{e}:{n - 2}"):
                ev.wait(vec, f"p3:acc:{e}:{n - 2}")
            elif ev.has(f"p3:acc:{e - 1}:{NT - 2 + n}"):
                ev.wait(vec, f"p3:acc:{e - 1}:{NT - 2 + n}")
            vec.inc(nc.vector.tensor_copy(h_sb[n % 2][:], pp[:]))
            ev.put(f"p3:h:{e}:{n}", vec)
        if e == E - 1:
            emit_acc(e, NT - 2)
            emit_acc(e, NT - 1)

    gps.wait(sG, gps.n)
    syn.wait(sS, syn.n)
    for n in range(NT):
        ev.wait(syn, f"p3:acc:{E - 1}:{n}")

    st3.close()
    ps3.close()
    st23.close()
    st.close()
    return nc


def _prep_weights(d):
    def re4(W):
        return np.ascontiguousarray(W.reshape(KT, P, NT, P).transpose(2, 1, 0, 3))

    def re5(W):
        return np.ascontiguousarray(W.reshape(E, KT, P, NT, P).transpose(0, 3, 2, 1, 4))

    def col(b):
        return np.ascontiguousarray(b.reshape(NT, P).T)

    import ml_dtypes
    identb = np.eye(P, dtype=ml_dtypes.bfloat16)
    identf = np.eye(P, dtype=np.float32)
    ii = np.arange(P)
    maskd = np.where(ii[None, :] <= ii[:, None], np.float32(0), np.float32(NEG)).astype(np.float32)
    return {
        "wq": re4(d["L0_attn_w_q"]), "wk": re4(d["L0_attn_w_k"]),
        "wv": np.ascontiguousarray(d["L0_attn_w_v"].reshape(KT, P, D)),
        "wo": re4(d["L0_attn_w_o"]),
        "bq": col(d["L0_attn_b_q"]), "bk": col(d["L0_attn_b_k"]),
        "bv": col(d["L0_attn_b_v"]), "bo": col(d["L0_attn_b_o"]),
        "wr": np.ascontiguousarray(d["L0_ffn_Wr"].reshape(KT, P, 8).transpose(1, 0, 2).reshape(P, P)),
        "wg": re5(d["L0_ffn_Wg"]), "wu": re5(d["L0_ffn_Wu"]), "wd": re5(d["L0_ffn_Wd"]),
        "identb": identb, "identf": identf, "maskd": maskd,
    }


def kernel(**inputs):
    x = np.asarray(inputs["L0_attn_query"], dtype=np.float32)
    shared = _prep_weights(inputs)
    nc = build()
    in_maps = []
    for c in range(8):
        m = dict(shared)
        m["xT"] = np.ascontiguousarray(x[c].T)
        in_maps.append(m)
    res = run_bass_kernel_spmd(nc, in_maps, core_ids=list(range(8)))
    out = np.empty((8, T, D), dtype=np.float32)
    for c in range(8):
        out[c] = np.asarray(res.results[c]["outT"]).T
    return out


# version 14 (final)
# speedup vs baseline: 1.0725x; 1.0725x over previous
import math
import numpy as np
from contextlib import ExitStack

import concourse.bass as bass
import concourse.mybir as mybir
from concourse.bass_utils import run_bass_kernel_spmd

F32 = mybir.dt.float32
BF16 = mybir.dt.bfloat16
AX = mybir.AxisListType.X
AF = mybir.ActivationFunctionType
ALU = mybir.AluOpType

P = 128
T = 2048
D = 2048
KT = 16
NT = 16
H = 16
CH = 4
E = 8
ISQ = 1.0 / math.sqrt(128.0)
EPS = 1e-5
NEG = -1.0e9


class Q:

    def __init__(self, eng, sem):
        self.eng = eng
        self.sem = sem
        self.n = 0
        self.waited = {}

    def inc(self, ins, k=1):
        self.n += k
        ins.then_inc(self.sem, k)
        return self.n

    def wait(self, sem, val):
        key = id(sem)
        if self.waited.get(key, -1) >= val:
            return
        self.eng.wait_ge(sem, val)
        self.waited[key] = val


class Ev:

    def __init__(self):
        self.d = {}

    def put(self, name, q):
        self.d[name] = (q.sem, q.n)

    def wait(self, q, name):
        s, v = self.d[name]
        q.wait(s, v)

    def has(self, name):
        return name in self.d


def build(debug=False):
    nc = bass.Bass("TRN2", target_bir_lowering=False)

    xT = nc.dram_tensor("xT", [D, T], F32, kind="ExternalInput").ap()
    wq = nc.dram_tensor("wq", [NT, P, KT, P], F32, kind="ExternalInput").ap()
    wk = nc.dram_tensor("wk", [NT, P, KT, P], F32, kind="ExternalInput").ap()
    wv = nc.dram_tensor("wv", [KT, P, D], F32, kind="ExternalInput").ap()
    wo = nc.dram_tensor("wo", [NT, P, KT, P], F32, kind="ExternalInput").ap()
    bq = nc.dram_tensor("bq", [P, NT], F32, kind="ExternalInput").ap()
    bk = nc.dram_tensor("bk", [P, NT], F32, kind="ExternalInput").ap()
    bv = nc.dram_tensor("bv", [P, NT], F32, kind="ExternalInput").ap()
    bo = nc.dram_tensor("bo", [P, NT], F32, kind="ExternalInput").ap()
    wr = nc.dram_tensor("wr", [P, P], F32, kind="ExternalInput").ap()
    wg = nc.dram_tensor("wg", [E, NT, P, KT, P], F32, kind="ExternalInput").ap()
    wu = nc.dram_tensor("wu", [E, NT, P, KT, P], F32, kind="ExternalInput").ap()
    wd = nc.dram_tensor("wd", [E, NT, P, KT, P], F32, kind="ExternalInput").ap()
    identb = nc.dram_tensor("identb", [P, P], BF16, kind="ExternalInput").ap()
    identf = nc.dram_tensor("identf", [P, P], F32, kind="ExternalInput").ap()
    maskd = nc.dram_tensor("maskd", [P, P], F32, kind="ExternalInput").ap()
    outT = nc.dram_tensor("outT", [D, T], F32, kind="ExternalOutput").ap()
    qT_s = nc.dram_tensor("qT_s", [D, T], BF16, kind="Internal").ap()
    kT_s = nc.dram_tensor("kT_s", [D, T], BF16, kind="Internal").ap()
    vN_s = nc.dram_tensor("vN_s", [T, D], BF16, kind="Internal").ap()
    gat_s = nc.dram_tensor("gat_s", [8, T], BF16, kind="Internal").ap()
    ln2_s = nc.dram_tensor("ln2_s", [D, T], BF16, kind="Internal").ap()
    if debug:
        d_ln1 = nc.dram_tensor("d_ln1", [P, KT * T], BF16, kind="ExternalOutput").ap()
        d_q = nc.dram_tensor("d_q", [D, T], BF16, kind="ExternalOutput").ap()
        d_k = nc.dram_tensor("d_k", [D, T], BF16, kind="ExternalOutput").ap()
        d_v = nc.dram_tensor("d_v", [T, D], BF16, kind="ExternalOutput").ap()
        d_ln2 = nc.dram_tensor("d_ln2", [P, KT * T], BF16, kind="ExternalOutput").ap()
        d_gat = nc.dram_tensor("d_gat", [8, T], BF16, kind="ExternalOutput").ap()
        d_att = nc.dram_tensor("d_att", [P, H * T], BF16, kind="ExternalOutput").ap()

    st = ExitStack()
    sb = lambda name, shape, dt: st.enter_context(nc.sbuf_tensor(name, shape, dt))

    sPE = st.enter_context(nc.semaphore("sPE"))
    sV = st.enter_context(nc.semaphore("sV"))
    sA = st.enter_context(nc.semaphore("sA"))
    sG = st.enter_context(nc.semaphore("sG"))
    sS = st.enter_context(nc.semaphore("sS"))

    id_bf = sb("id_bf", [P, P], BF16)
    id_f = sb("id_f", [P, P], F32)
    mask_sb = sb("mask_sb", [P, P], F32)
    bq_sb = sb("bq_sb", [P, NT], F32)
    bk_sb = sb("bk_sb", [P, NT], F32)
    bv_sb = sb("bv_sb", [P, NT], F32)
    bo_sb = sb("bo_sb", [P, NT], F32)
    bqs_sb = sb("bqs_sb", [P, NT], F32)
    wr_bf = sb("wr_bf", [P, P], BF16)
    ones_c = sb("ones_c", [P, 1], BF16)
    ones_r = sb("ones_r", [1, P], BF16)
    wseg = [sb(f"wseg{i}", [P, KT * P], BF16) for i in range(2)]

    block = st.enter_context(nc.Block())

    tens = Q(nc.tensor, sPE)
    vec = Q(nc.vector, sV)
    act = Q(nc.scalar, sA)
    gps = Q(nc.gpsimd, sG)
    syn = Q(nc.sync, sS)
    ev = Ev()

    def vself(ins):
        vec.inc(ins)
        vec.wait(sV, vec.n)

    def gself(ins):
        gps.inc(ins)
        gps.wait(sG, gps.n)

    syn.inc(nc.sync.dma_start(id_bf[:], identb), 16)
    syn.inc(nc.sync.dma_start(id_f[:], identf), 16)
    syn.inc(nc.sync.dma_start(mask_sb[:], maskd), 16)
    syn.inc(nc.sync.dma_start(bq_sb[:], bq), 16)
    syn.inc(nc.sync.dma_start(bk_sb[:], bk), 16)
    syn.inc(nc.sync.dma_start(bv_sb[:], bv), 16)
    syn.inc(nc.sync.dma_start(bo_sb[:], bo), 16)
    ev.put("c:bias", syn)
    gps.inc(nc.gpsimd.dma_start(wr_bf[:], wr), 16)
    ev.put("c:wr", gps)
    vec.inc(nc.vector.memset(ones_c[:], 1.0))
    vec.inc(nc.vector.memset(ones_r[:], 1.0))
    ev.put("c:ones", vec)
    ev.wait(vec, "c:bias")
    vec.inc(nc.vector.tensor_scalar_mul(bqs_sb[:], bq_sb[:], ISQ))
    ev.put("c:bqs", vec)

    st1 = ExitStack()
    sb1 = lambda name, shape, dt: st1.enter_context(nc.sbuf_tensor(name, shape, dt))
    xt = sb1("xt", [P, KT * T], BF16)
    ln1 = sb1("ln1", [P, KT * T], BF16)
    tmp1 = [sb1(f"tmp1_{i}", [P, T], BF16) for i in range(2)]
    stage = [sb1(f"stage{i}", [P, T], BF16) for i in range(2)]
    rows = sb1("rows", [1, 3 * T], F32)
    rows_bf = sb1("rows_bf", [1, 2 * T], BF16)
    m_b = sb1("m_b", [P, T], F32)
    s_b = sb1("s_b", [P, T], F32)

    for k in range(KT):
        gps.inc(nc.gpsimd.dma_start(xt[:, k * T:(k + 1) * T], xT[k * P:(k + 1) * P, :]), 16)
        ev.put(f"p1:xt{k}", gps)
    for k in range(KT):
        ev.wait(gps, f"p1:xt{k}")
        gps.inc(nc.gpsimd.tensor_tensor(ln1[:, k * T:(k + 1) * T], xt[:, k * T:(k + 1) * T],
                                        xt[:, k * T:(k + 1) * T], ALU.mult))
        ev.put(f"p1:sq{k}", gps)

    stps = ExitStack()
    st_ps = [stps.enter_context(nc.psum_tensor(f"st_ps{j}", [P, 512], F32)) for j in range(8)]
    ev.wait(tens, "c:ones")
    ev.wait(tens, "p1:sq15")
    for c in range(CH):
        for s_i in range(2):
            src = xt if s_i == 0 else ln1
            for k in range(KT):
                ins = nc.tensor.matmul(st_ps[2 * c + s_i][0:1, :], ones_c[:],
                                       src[:, k * T + c * 512: k * T + (c + 1) * 512],
                                       start=(k == 0), stop=(k == KT - 1))
            tens.inc(ins)
            ev.put(f"p1:st{2 * c + s_i}", tens)
    for c in range(CH):
        ev.wait(vec, f"p1:st{2 * c}")
        vec.inc(nc.vector.tensor_copy(rows[0:1, c * 512:(c + 1) * 512], st_ps[2 * c][0:1, :]))
        ev.wait(vec, f"p1:st{2 * c + 1}")
        vec.inc(nc.vector.tensor_copy(rows[0:1, T + c * 512: T + (c + 1) * 512],
                                      st_ps[2 * c + 1][0:1, :]))
    ev.put("p1:stev", vec)
    vec.wait(sV, vec.n)
    r_m = rows[0:1, 0:T]
    r_v = rows[0:1, T:2 * T]
    r_t = rows[0:1, 2 * T:3 * T]
    vself(nc.vector.tensor_scalar_mul(r_m, r_m, 1.0 / D))
    vself(nc.vector.tensor_scalar_mul(r_v, r_v, 1.0 / D))
    vself(nc.vector.tensor_tensor(r_t, r_m, r_m, ALU.mult))
    vself(nc.vector.tensor_tensor(r_v, r_v, r_t, ALU.subtract))
    vself(nc.vector.tensor_scalar_add(r_v, r_v, EPS))
    vself(nc.vector.reciprocal(r_t, r_v))
    ev.put("p1:rowsv", vec)
    ev.wait(act, "p1:rowsv")
    ins = nc.scalar.activation(r_v, r_t, AF.Sqrt)
    act.inc(ins)
    ev.put("p1:srow", act)
    ev.wait(vec, "p1:srow")
    vself(nc.vector.tensor_copy(rows_bf[0:1, 0:T], r_m))
    vself(nc.vector.tensor_copy(rows_bf[0:1, T:2 * T], r_v))
    ev.put("p1:rowsbf", vec)
    stps.close()

    ps1 = ExitStack()
    ps_a = ps1.enter_context(nc.psum_tensor("ps_a", [P, T], F32))
    ps_b = ps1.enter_context(nc.psum_tensor("ps_b", [P, T], F32))
    ev.wait(tens, "p1:rowsbf")
    for c in range(CH):
        ins = nc.tensor.matmul(ps_a[:, c * 512:(c + 1) * 512], ones_r[:],
                               rows_bf[0:1, c * 512:(c + 1) * 512], start=True, stop=True)
        tens.inc(ins)
        ins = nc.tensor.matmul(ps_b[:, c * 512:(c + 1) * 512], ones_r[:],
                               rows_bf[0:1, T + c * 512: T + (c + 1) * 512], start=True, stop=True)
        tens.inc(ins)
    ev.put("p1:bcast", tens)
    ev.wait(vec, "p1:bcast")
    vec.inc(nc.vector.tensor_copy(m_b[:], ps_a[:]))
    vec.inc(nc.vector.tensor_copy(s_b[:], ps_b[:]))
    ev.put("p1:mbsb", vec)
    vec.wait(sV, vec.n)

    for k in range(KT):
        vself(nc.vector.tensor_tensor(tmp1[k % 2][:], xt[:, k * T:(k + 1) * T], m_b[:], ALU.subtract))
        vec.inc(nc.vector.tensor_tensor(ln1[:, k * T:(k + 1) * T], tmp1[k % 2][:], s_b[:], ALU.mult))
        ev.put(f"p1:ln1_{k}", vec)
    ev.put("p1:ln1", vec)

    def qk_pass(tag, wdram, dst, bias_ap, scale):
        for n in range(NT):
            if ev.has(f"{tag}:mm{n - 2}"):
                ev.wait(gps, f"{tag}:mm{n - 2}")
            elif tag == "k" and n < 2:
                ev.wait(gps, f"q:mm{NT - 2 + n}")
            gps.inc(nc.gpsimd.dma_start(wseg[n % 2][:], wdram[n].rearrange("p k j -> p (k j)")), 16)
            ev.put(f"{tag}:w{n}", gps)
            if tag == "k":
                ev.wait(gps, f"p1:ln1_{n}")
                gps.inc(nc.gpsimd.dma_start(xt[:, n * T:(n + 1) * T], wv[n]), 16)
                ev.put(f"v:w{n}", gps)
            pp = ps_a if n % 2 == 0 else ps_b
            ev.wait(tens, f"{tag}:w{n}")
            if ev.has(f"{tag}:ev{n - 2}"):
                ev.wait(tens, f"{tag}:ev{n - 2}")
            elif tag == "k" and n < 2:
                ev.wait(tens, f"q:ev{NT - 2 + n}")
            for k in range(KT):
                ev.wait(tens, f"p1:ln1_{k}")
                for c in range(CH):
                    ins = nc.tensor.matmul(pp[:, c * 512:(c + 1) * 512],
                                           wseg[n % 2][:, k * P:(k + 1) * P],
                                           ln1[:, k * T + c * 512: k * T + (c + 1) * 512],
                                           start=(k == 0), stop=(k == KT - 1))
            tens.inc(ins)
            ev.put(f"{tag}:mm{n}", tens)
            ev.wait(act, f"{tag}:mm{n}")
            if ev.has(f"{tag}:st{n - 2}"):
                ev.wait(act, f"{tag}:st{n - 2}")
            elif tag == "k" and n < 2:
                ev.wait(act, f"q:st{NT - 2 + n}")
            ins = nc.scalar.activation(stage[n % 2][:], pp[:], AF.Identity,
                                       bias=bias_ap[:, n:n + 1], scale=scale)
            act.inc(ins)
            ev.put(f"{tag}:ev{n}", act)
            ev.wait(syn, f"{tag}:ev{n}")
            syn.inc(nc.sync.dma_start(dst[n * P:(n + 1) * P, :], stage[n % 2][:]), 16)
            ev.put(f"{tag}:st{n}", syn)

    ev.wait(act, "c:bqs")
    qk_pass("q", wq, qT_s, bqs_sb, ISQ)
    qk_pass("k", wk, kT_s, bk_sb, 1.0)

    for t in range(NT):
        pp = ps_a if t % 2 == 0 else ps_b
        ev.wait(tens, f"v:w{KT - 1}")
        if ev.has(f"v:ev{t - 2}"):
            ev.wait(tens, f"v:ev{t - 2}")
        elif t < 2:
            ev.wait(tens, f"k:ev{NT - 2 + t}")
        for k in range(KT):
            for c in range(CH):
                ins = nc.tensor.matmul(pp[:, c * 512:(c + 1) * 512],
                                       ln1[:, k * T + t * P: k * T + (t + 1) * P],
                                       xt[:, k * T + c * 512: k * T + (c + 1) * 512],
                                       start=(k == 0), stop=(k == KT - 1))
        tens.inc(ins)
        ev.put(f"v:mm{t}", tens)
        ev.wait(act, f"v:mm{t}")
        if ev.has(f"v:st{t - 2}"):
            ev.wait(act, f"v:st{t - 2}")
        elif t < 2:
            ev.wait(act, f"k:st{NT - 2 + t}")
        ins = nc.scalar.activation(stage[t % 2][:], pp[:], AF.Identity)
        act.inc(ins)
        ev.put(f"v:ev{t}", act)
        ev.wait(syn, f"v:ev{t}")
        syn.inc(nc.sync.dma_start(vN_s[t * P:(t + 1) * P, :], stage[t % 2][:]), 16)
        ev.put(f"v:st{t}", syn)

    if debug:
        ev.wait(syn, "p1:ln1")
        syn.inc(nc.sync.dma_start(d_ln1, ln1[:]), 16)
        ev.wait(syn, f"q:st{NT - 1}")
        ev.wait(syn, f"k:st{NT - 1}")
        ev.wait(syn, f"v:st{NT - 1}")
        syn.inc(nc.sync.dma_start(d_q, qT_s), 16)
        syn.inc(nc.sync.dma_start(d_k, kT_s), 16)
        syn.inc(nc.sync.dma_start(d_v, vN_s), 16)

    p1_done = [f"v:ev{NT - 1}", f"v:st{NT - 1}", f"v:mm{NT - 1}", "p1:ln1",
               f"q:st{NT - 1}", f"k:st{NT - 1}", f"v:w{KT - 1}"]
    st1.close()
    ps1.close()

    st23 = ExitStack()
    sb23 = lambda name, shape, dt: st23.enter_context(nc.sbuf_tensor(name, shape, dt))


    st2 = ExitStack()
    sb2 = lambda name, shape, dt: st2.enter_context(nc.sbuf_tensor(name, shape, dt))
    qc = [sb2(f"qc{i}", [P, 512], BF16) for i in range(4)]
    kc = [sb2(f"kc{i}", [P, T], BF16) for i in range(4)]
    vc = [sb2(f"vc{i}", [P, KT * P], BF16) for i in range(4)]
    p_sb = [sb2(f"p_sb{i}", [P, T], BF16) for i in range(4)]
    p2 = [sb2(f"p2_{i}", [P, T], BF16) for i in range(4)]
    pT = sb2("pT", [P, KT * 512], BF16)
    attnT = sb2("attnT", [P, H * 512], BF16)
    x1c = sb2("x1c", [P, KT * 512], F32)
    xin = [sb2(f"xin{i}", [P, 512], F32) for i in range(2)]
    x1bf = [sb2(f"x1bf{i}", [P, 512], BF16) for i in range(2)]
    x2t = [sb2(f"x2t{i}", [P, 512], BF16) for i in range(2)]
    m2b = sb2("m2b", [P, 512], F32)
    s2b = sb2("s2b", [P, 512], F32)
    rows2 = sb2("rows2", [1, 3 * 512], F32)
    rows2bf = sb2("rows2bf", [1, 1024], BF16)
    nm = sb2("nm", [P, 24], F32)
    rs = sb2("rs", [P, 16], F32)
    ri = sb2("ri", [P, 8], F32)
    lgt = sb2("lgt", [8, 512], F32)
    lgn = sb2("lgn", [P, 32], F32)
    gw = sb2("gw", [P, 8], F32)
    gw2 = sb2("gw2", [P, 8], F32)
    gts = sb2("gts", [8, 512], BF16)
    ln2c = sb2("ln2c", [P, KT * 512], BF16)
    gm = sb2("gm", [P, 4], F32)
    gn = sb2("gn", [P, 32], F32)

    ps2 = ExitStack()
    ps_sc = ps2.enter_context(nc.psum_tensor("ps_sc", [P, 3072], F32))
    ps_tr = ps2.enter_context(nc.psum_tensor("ps_tr", [P, 1024], BF16))
    ps_av = ps2.enter_context(nc.psum_tensor("ps_av", [P, 512], F32))

    for qq in (tens, vec, act, gps, syn):
        for e in p1_done:
            ev.wait(qq, e)

    last_exp = [None, None]
    last_trev = [None, None]
    last_p_user = [None] * 4
    last_p2tr = [None] * 4
    last_pexp = [None] * 3
    pcn = [0]
    pending_stores = []

    def flush_stores():
        while pending_stores:
            kind, cc, n = pending_stores.pop(0)
            if kind == "ln2":
                ev.wait(syn, f"p2:ln2:{cc}:{n}")
                syn.inc(nc.sync.dma_start(ln2_s[n * P:(n + 1) * P, cc * 512:(cc + 1) * 512],
                                          ln2c[:, n * 512:(n + 1) * 512]), 16)
                ev.put(f"p2:ln2st:{cc}:{n}", syn)
            else:
                ev.wait(syn, f"p2:gates:{cc}")
                syn.inc(nc.sync.dma_start(gat_s[0:8, cc * 512:(cc + 1) * 512], gts[:]), 16)
                ev.put(f"p2:gatst:{cc}", syn)

    def emit_router(cc):
        s = pcn[0] % 3
        pcn[0] += 1
        off = s * 1024
        ev.wait(tens, f"p2:ln2:{cc}")
        ev.wait(tens, "c:wr")
        if last_pexp[s] is not None:
            ev.wait(tens, last_pexp[s])
        for k in range(KT):
            ins = nc.tensor.matmul(ps_sc[0:8, off:off + 512], wr_bf[:, k * 8:(k + 1) * 8],
                                   ln2c[:, k * 512:(k + 1) * 512],
                                   start=(k == 0), stop=(k == KT - 1))
        tens.inc(ins)
        ev.put(f"p2:lg:{cc}", tens)
        ev.wait(vec, f"p2:lg:{cc}")
        vself(nc.vector.tensor_copy(lgt[:], ps_sc[0:8, off:off + 512]))
        ev.put(f"p2:lgt:{cc}", vec)
        last_pexp[s] = f"p2:lgt:{cc}"
        s2_ = pcn[0] % 3
        pcn[0] += 1
        off2 = s2_ * 1024
        ev.wait(tens, f"p2:lgt:{cc}")
        if last_pexp[s2_] is not None:
            ev.wait(tens, last_pexp[s2_])
        for u in range(4):
            ins = nc.tensor.transpose(ps_sc[:, off2 + u * 8: off2 + (u + 1) * 8],
                                      lgt[0:8, u * P:(u + 1) * P], id_f[0:8, 0:8])
        tens.inc(ins)
        ev.put(f"p2:lgtr:{cc}", tens)
        ev.wait(vec, f"p2:lgtr:{cc}")
        vself(nc.vector.tensor_copy(lgn[:], ps_sc[:, off2:off2 + 32]))
        ev.put(f"p2:lgn:{cc}", vec)
        last_pexp[s2_] = f"p2:lgn:{cc}"
        for u in range(4):
            lu = lgn[:, u * 8:(u + 1) * 8]
            vself(nc.vector.reduce_max(gm[:, 0:1], lu, axis=AX))
            vself(nc.vector.tensor_single_scalar(gw[:], lu, gm[:, 0:1], ALU.is_equal))
            vself(nc.vector.scalar_tensor_tensor(gw2[:], gw[:], 1.0e30, lu, ALU.mult, ALU.subtract))
            vself(nc.vector.tensor_reduce(gm[:, 1:2], gw2[:], axis=AX, op=ALU.min, negate=True))
            vself(nc.vector.tensor_tensor(gm[:, 2:3], gm[:, 0:1], gm[:, 1:2], ALU.subtract))
            ev.put(f"p2:gd:{cc}:{u}", vec)
            ev.wait(act, f"p2:gd:{cc}:{u}")
            act.inc(nc.scalar.activation(gm[:, 2:3], gm[:, 2:3], AF.Sigmoid))
            ev.put(f"p2:gp:{cc}:{u}", act)
            ev.wait(vec, f"p2:gp:{cc}:{u}")
            vself(nc.vector.tensor_scalar(gm[:, 3:4], gm[:, 2:3], -1.0, 1.0, ALU.mult, ALU.add))
            vself(nc.vector.tensor_single_scalar(gw2[:], lu, gm[:, 1:2], ALU.is_equal))
            vself(nc.vector.tensor_scalar_mul(gw[:], gw[:], gm[:, 2:3]))
            vself(nc.vector.tensor_scalar_mul(gw2[:], gw2[:], gm[:, 3:4]))
            vself(nc.vector.tensor_tensor(gn[:, u * 8:(u + 1) * 8], gw[:], gw2[:], ALU.add))
        ev.put(f"p2:gn:{cc}", vec)
        s3_ = pcn[0] % 3
        pcn[0] += 1
        off3 = s3_ * 1024
        ev.wait(tens, f"p2:gn:{cc}")
        if last_pexp[s3_] is not None:
            ev.wait(tens, last_pexp[s3_])
        for u in range(4):
            ins = nc.tensor.transpose(ps_sc[0:8, off3 + u * P: off3 + (u + 1) * P],
                                      gn[:, u * 8:(u + 1) * 8], id_f[:])
        tens.inc(ins)
        ev.put(f"p2:gtr:{cc}", tens)
        ev.wait(vec, f"p2:gtr:{cc}")
        if ev.has(f"p2:gatst:{cc - 1}"):
            ev.wait(vec, f"p2:gatst:{cc - 1}")
        vec.inc(nc.vector.tensor_copy(gts[:], ps_sc[0:8, off3:off3 + 512]))
        ev.put(f"p2:gates:{cc}", vec)
        last_pexp[s3_] = f"p2:gates:{cc}"
        pending_stores.append(("gat", cc, 0))
    for c in range(CH):
        JT = 4 * c + 4
        L_MAX = JT * P
        def emit_loads(h):
            if ev.has(f"p2:sc:{c}:{h - 4}:3"):
                ev.wait(syn, f"p2:sc:{c}:{h - 4}:3")
            elif c > 0 and h < 4:
                ev.wait(syn, f"p2:sc:{c - 1}:{H - 4 + h}:3")
            else:
                ev.wait(syn, f"q:st{NT - 1}")
            syn.inc(nc.sync.dma_start(qc[h % 4][:], qT_s[h * P:(h + 1) * P, c * 512:(c + 1) * 512]), 16)
            syn.inc(nc.sync.dma_start(kc[h % 4][:, 0:L_MAX], kT_s[h * P:(h + 1) * P, 0:L_MAX]), 16)
            ev.put(f"p2:kc:{c}:{h}", syn)
            if ev.has(f"p2:av:{c}:{h - 4}"):
                ev.wait(act, f"p2:av:{c}:{h - 4}")
            elif c > 0 and h < 4:
                ev.wait(act, f"p2:av:{c - 1}:{H - 4 + h}")
            act.inc(nc.scalar.dma_start(
                vc[h % 4][:, 0:L_MAX].rearrange("p (j d) -> p j d", d=P),
                vN_s.rearrange("(j p) d -> p j d", p=P)[:, 0:JT, h * P:(h + 1) * P]), 16)
            ev.put(f"p2:vc:{c}:{h}", act)

        def emit_scores(h, bs):
            for b in bs:
                tg = 4 * c + b
                L = (tg + 1) * P
                L0 = min(L, 1024)
                two = L > 1024
                slot = (h % 2) * 2 + (b % 2)
                cA = (h % 2) * 4 + b
                s0 = pcn[0] % 3
                pcn[0] += 1
                o0 = s0 * 1024
                ev.wait(tens, f"p2:kc:{c}:{h}")
                if last_pexp[s0] is not None:
                    ev.wait(tens, last_pexp[s0])
                for j0 in range(0, L0, 512):
                    wdt = min(512, L0 - j0)
                    ins = nc.tensor.matmul(ps_sc[:, o0 + j0: o0 + j0 + wdt],
                                           qc[h % 4][:, b * P:(b + 1) * P],
                                           kc[h % 4][:, j0: j0 + wdt],
                                           start=True, stop=True)
                tens.inc(ins)
                ev.put(f"p2:sc0:{c}:{h}:{b}", tens)
                if two:
                    s1 = pcn[0] % 3
                    pcn[0] += 1
                    o1 = s1 * 1024
                    if last_pexp[s1] is not None:
                        ev.wait(tens, last_pexp[s1])
                    for j0 in range(1024, L, 512):
                        wdt = min(512, L - j0)
                        ins = nc.tensor.matmul(ps_sc[:, o1 + j0 - 1024: o1 + j0 - 1024 + wdt],
                                               qc[h % 4][:, b * P:(b + 1) * P],
                                               kc[h % 4][:, j0: j0 + wdt],
                                               start=True, stop=True)
                    tens.inc(ins)
                ev.put(f"p2:sc:{c}:{h}:{b}", tens)
                if ev.has(f"p2:exp:{c}:{h - 2}:{b}"):
                    ev.wait(vec, f"p2:exp:{c}:{h - 2}:{b}")
                elif c > 0 and h < 2 and ev.has(f"p2:exp:{c - 1}:{H - 2 + h}:{b}"):
                    ev.wait(vec, f"p2:exp:{c - 1}:{H - 2 + h}:{b}")
                if not two:
                    ev.wait(vec, f"p2:sc0:{c}:{h}:{b}")
                    vec.inc(nc.vector.tensor_tensor(ps_sc[:, o0 + tg * P: o0 + L],
                                                    ps_sc[:, o0 + tg * P: o0 + L],
                                                    mask_sb[:], ALU.add))
                    vec.wait(sV, vec.n)
                    vec.inc(nc.vector.reduce_max(nm[:, cA:cA + 1], ps_sc[:, o0: o0 + L],
                                                 axis=AX, negate=True))
                    ev.put(f"p2:max:{c}:{h}:{b}", vec)
                else:
                    ev.wait(vec, f"p2:sc0:{c}:{h}:{b}")
                    vec.inc(nc.vector.reduce_max(nm[:, 8 + cA:9 + cA], ps_sc[:, o0: o0 + 1024],
                                                 axis=AX, negate=True))
                    ev.wait(vec, f"p2:sc:{c}:{h}:{b}")
                    vec.inc(nc.vector.tensor_tensor(ps_sc[:, o1 + tg * P - 1024: o1 + L - 1024],
                                                    ps_sc[:, o1 + tg * P - 1024: o1 + L - 1024],
                                                    mask_sb[:], ALU.add))
                    vec.wait(sV, vec.n)
                    vself(nc.vector.reduce_max(nm[:, 16 + cA:17 + cA],
                                               ps_sc[:, o1: o1 + L - 1024],
                                               axis=AX, negate=True))
                    vec.inc(nc.vector.tensor_tensor(nm[:, cA:cA + 1], nm[:, 8 + cA:9 + cA],
                                                    nm[:, 16 + cA:17 + cA], ALU.min))
                    ev.put(f"p2:max:{c}:{h}:{b}", vec)
                ev.wait(act, f"p2:max:{c}:{h}:{b}")
                if last_p_user[slot] is not None:
                    ev.wait(act, last_p_user[slot])
                if ev.has(f"p2:ri:{c}:{h - 2}:{b}"):
                    ev.wait(act, f"p2:ri:{c}:{h - 2}:{b}")
                ins = nc.scalar.activation(p_sb[slot][:, 0:L0], ps_sc[:, o0: o0 + L0], AF.Exp,
                                           bias=nm[:, cA:cA + 1], scale=1.0,
                                           accum_out=rs[:, cA:cA + 1])
                act.inc(ins)
                ev.put(f"p2:exp0:{c}:{h}:{b}", act)
                last_pexp[s0] = f"p2:exp0:{c}:{h}:{b}"
                if two:
                    ins = nc.scalar.activation(p_sb[slot][:, 1024:L], ps_sc[:, o1: o1 + L - 1024],
                                               AF.Exp, bias=nm[:, cA:cA + 1], scale=1.0,
                                               accum_out=rs[:, 8 + cA:9 + cA])
                    act.inc(ins)
                    ev.put(f"p2:exp:{c}:{h}:{b}", act)
                    last_pexp[s1] = f"p2:exp:{c}:{h}:{b}"
                else:
                    ev.put(f"p2:exp:{c}:{h}:{b}", act)

        def emit_ptz(h):
            for b in range(4):
                tg = 4 * c + b
                if tg + 1 < JT:
                    zap = pT[:].rearrange("p (j u) -> p j u", u=512)[:, tg + 1:JT, b * P:(b + 1) * P]
                    if ev.has(f"p2:avdone:{c}:{h - 1}"):
                        ev.wait(vec, f"p2:avdone:{c}:{h - 1}")
                    elif c > 0 and h == 0:
                        ev.wait(vec, f"p2:avdone:{c - 1}:{H - 1}")
                    vec.inc(nc.vector.memset(zap, 0.0))
                    ev.put(f"p2:ptz:{c}:{h}:{b}", vec)

        def emit_tr(h, bs):
            for b in bs:
                tg = 4 * c + b
                L = (tg + 1) * P
                slot = (h % 2) * 2 + (b % 2)
                cA = (h % 2) * 4 + b
                two = L > 1024
                ev.wait(vec, f"p2:exp:{c}:{h}:{b}")
                if two:
                    vself(nc.vector.tensor_tensor(rs[:, cA:cA + 1], rs[:, cA:cA + 1],
                                                  rs[:, 8 + cA:9 + cA], ALU.add))
                vself(nc.vector.reciprocal(ri[:, cA:cA + 1], rs[:, cA:cA + 1]))
                ev.put(f"p2:ri:{c}:{h}:{b}", vec)
                if last_p2tr[slot] is not None:
                    ev.wait(vec, last_p2tr[slot])
                vec.inc(nc.vector.tensor_scalar_mul(p2[slot][:, 0:L], p_sb[slot][:, 0:L],
                                                    ri[:, cA:cA + 1]))
                ev.put(f"p2:pmul:{c}:{h}:{b}", vec)
                last_p_user[slot] = f"p2:pmul:{c}:{h}:{b}"
                ntile = tg + 1
                rr = 0
                for r0 in range(0, ntile, 8):
                    slots = min(8, ntile - r0)
                    ev.wait(tens, f"p2:pmul:{c}:{h}:{b}")
                    if last_trev[0] is not None:
                        ev.wait(tens, last_trev[0])
                    for j in range(r0, r0 + slots):
                        ins = nc.tensor.transpose(
                            ps_tr[:, (j - r0) * P: (j - r0 + 1) * P],
                            p2[slot][:, j * P:(j + 1) * P], id_bf[:])
                    tens.inc(ins)
                    ev.put(f"p2:tr:{c}:{h}:{b}:{rr}", tens)
                    eng = vec if b % 2 == 0 else act
                    ev.wait(eng, f"p2:tr:{c}:{h}:{b}:{rr}")
                    if ev.has(f"p2:avdone:{c}:{h - 1}"):
                        ev.wait(eng, f"p2:avdone:{c}:{h - 1}")
                    elif c > 0 and h == 0:
                        ev.wait(eng, f"p2:avdone:{c - 1}:{H - 1}")
                    dst = pT[:].rearrange("p (j u) -> p j u", u=512)[:, r0:r0 + slots, b * P:(b + 1) * P]
                    srcv = ps_tr[:, 0: slots * P].rearrange("p (j d) -> p j d", d=P)
                    if eng is vec:
                        eng.inc(nc.vector.tensor_copy(dst, srcv))
                    else:
                        eng.inc(nc.scalar.copy(dst, srcv))
                    ev.put(f"p2:trev:{c}:{h}:{b}:{rr}", eng)
                    last_trev[0] = f"p2:trev:{c}:{h}:{b}:{rr}"
                    rr += 1
                last_p2tr[slot] = f"p2:tr:{c}:{h}:{b}:{rr - 1}"
                ev.put(f"p2:trevl:{c}:{h}:{b}", eng)

        def emit_av(h):
            for b in range(4):
                ev.wait(tens, f"p2:trevl:{c}:{h}:{b}")
            for b in range(3):
                if ev.has(f"p2:ptz:{c}:{h}:{b}"):
                    ev.wait(tens, f"p2:ptz:{c}:{h}:{b}")
            ev.wait(tens, f"p2:vc:{c}:{h}")
            if ev.has(f"p2:av:{c}:{h - 1}"):
                ev.wait(tens, f"p2:av:{c}:{h - 1}")
            elif c > 0 and h == 0:
                ev.wait(tens, f"p2:av:{c - 1}:{NT - 1}")
                ev.wait(tens, f"p2:m2b:{c - 1}")
            for j in range(JT):
                ins = nc.tensor.matmul(ps_av[:, 0:512],
                                       vc[h % 4][:, j * P:(j + 1) * P],
                                       pT[:, j * 512:(j + 1) * 512],
                                       start=(j == 0), stop=(j == JT - 1))
            tens.inc(ins)
            ev.put(f"p2:avmm:{c}:{h}", tens)
            ev.put(f"p2:avdone:{c}:{h}", tens)
            ev.wait(act, f"p2:avmm:{c}:{h}")
            if c > 0 and h == 0:
                ev.wait(act, f"p2:op:{c - 1}:{NT - 1}")
            ins = nc.scalar.activation(attnT[:, h * 512:(h + 1) * 512],
                                       ps_av[:, 0:512],
                                       AF.Identity, bias=bv_sb[:, h:h + 1])
            act.inc(ins)
            ev.put(f"p2:av:{c}:{h}", act)

        for n in range(2):
            if c > 0:
                ev.wait(gps, f"p2:op:{c - 1}:{NT - 2 + n}")
            gps.inc(nc.gpsimd.dma_start(wseg[n % 2][:], wo[n].rearrange("p k j -> p (k j)")), 16)
            ev.put(f"p2:wo:{c}:{n}", gps)

        for hh in range(H + 2):
            if hh < H:
                emit_loads(hh)
            if hh == 6:
                flush_stores()
            if hh == 7 and c > 0:
                emit_router(c - 1)
            if hh >= 2:
                emit_ptz(hh - 2)
            if hh >= 2:
                emit_tr(hh - 2, [0, 1, 2, 3])
            if hh < H:
                emit_scores(hh, [0, 1, 2, 3])
            if hh >= 2:
                emit_av(hh - 2)

        if debug:
            ev.wait(syn, f"p2:av:{c}:{H - 1}")
            syn.inc(nc.sync.dma_start(
                d_att.rearrange("p (h z u) -> p h z u", z=CH, u=512)[:, :, c, :],
                attnT[:].rearrange("p (h u) -> p h u", u=512)), 16)

        for n in range(NT):
            if n >= 2:
                ev.wait(gps, f"p2:op:{c}:{n - 2}")
                gps.inc(nc.gpsimd.dma_start(wseg[n % 2][:], wo[n].rearrange("p k j -> p (k j)")), 16)
                ev.put(f"p2:wo:{c}:{n}", gps)
            if ev.has(f"p2:x1:{c}:{n - 2}"):
                ev.wait(syn, f"p2:x1:{c}:{n - 2}")
            elif c > 0 and n < 2:
                ev.wait(syn, f"p2:x1:{c - 1}:{NT - 2 + n}")
            syn.inc(nc.sync.dma_start(xin[n % 2][:], xT[n * P:(n + 1) * P, c * 512:(c + 1) * 512]), 16)
            ev.put(f"p2:xin:{c}:{n}", syn)
            ev.wait(tens, f"p2:wo:{c}:{n}")
            ev.wait(tens, f"p2:av:{c}:{H - 1}")
            if ev.has(f"p2:x1:{c}:{n - 1}"):
                ev.wait(tens, f"p2:x1:{c}:{n - 1}")
            for k in range(KT):
                ins = nc.tensor.matmul(ps_av[:, 0:512],
                                       wseg[n % 2][:, k * P:(k + 1) * P],
                                       attnT[:, k * 512:(k + 1) * 512],
                                       start=(k == 0), stop=(k == KT - 1))
            tens.inc(ins)
            ev.put(f"p2:op:{c}:{n}", tens)
            ev.wait(vec, f"p2:op:{c}:{n}")
            ev.wait(vec, f"p2:xin:{c}:{n}")
            vec.inc(nc.vector.scalar_tensor_tensor(
                x1c[:, n * 512:(n + 1) * 512], xin[n % 2][:], bo_sb[:, n:n + 1],
                ps_av[:, 0:512], ALU.add, ALU.add))
            ev.put(f"p2:x1:{c}:{n}", vec)
            ev.wait(act, f"p2:x1:{c}:{n}")
            act.inc(nc.scalar.dma_start(outT[n * P:(n + 1) * P, c * 512:(c + 1) * 512],
                                        x1c[:, n * 512:(n + 1) * 512]), 16)
            ev.put(f"p2:x1st:{c}:{n}", act)

        for n in range(NT):
            if ev.has(f"p2:stpe:{c}:{n - 2}"):
                ev.wait(vec, f"p2:stpe:{c}:{n - 2}")
            vself(nc.vector.tensor_copy(x1bf[n % 2][:], x1c[:, n * 512:(n + 1) * 512]))
            vec.inc(nc.vector.tensor_tensor(x2t[n % 2][:], x1bf[n % 2][:], x1bf[n % 2][:], ALU.mult))
            ev.put(f"p2:x2t:{c}:{n}", vec)
            ev.wait(tens, f"p2:x2t:{c}:{n}")
            if n == 0:
                ev.wait(tens, f"p2:x1:{c}:{NT - 1}")
            ins = nc.tensor.matmul(ps_av[0:1, 0:512], ones_c[:], x1bf[n % 2][:],
                                   start=(n == 0), stop=(n == NT - 1))
            ins2 = nc.tensor.matmul(ps_sc[0:1, 0:512], ones_c[:], x2t[n % 2][:],
                                    start=(n == 0), stop=(n == NT - 1))
            tens.inc(ins2)
            ev.put(f"p2:stpe:{c}:{n}", tens)
        ev.put(f"p2:st2:{c}", tens)

        ev.wait(vec, f"p2:st2:{c}")
        vself(nc.vector.tensor_copy(rows2[0:1, 0:512], ps_av[0:1, 0:512]))
        vself(nc.vector.tensor_copy(rows2[0:1, 512:1024], ps_sc[0:1, 0:512]))
        r2m = rows2[0:1, 0:512]
        r2v = rows2[0:1, 512:1024]
        r2t = rows2[0:1, 1024:1536]
        vself(nc.vector.tensor_scalar_mul(r2m, r2m, 1.0 / D))
        vself(nc.vector.tensor_scalar_mul(r2v, r2v, 1.0 / D))
        vself(nc.vector.tensor_tensor(r2t, r2m, r2m, ALU.mult))
        vself(nc.vector.tensor_tensor(r2v, r2v, r2t, ALU.subtract))
        vself(nc.vector.tensor_scalar_add(r2v, r2v, EPS))
        vself(nc.vector.reciprocal(r2t, r2v))
        ev.put(f"p2:r2v:{c}", vec)
        ev.wait(act, f"p2:r2v:{c}")
        act.inc(nc.scalar.activation(r2v, r2t, AF.Sqrt))
        ev.put(f"p2:r2s:{c}", act)
        ev.wait(vec, f"p2:r2s:{c}")
        vself(nc.vector.tensor_copy(rows2bf[0:1, 0:512], r2m))
        vself(nc.vector.tensor_copy(rows2bf[0:1, 512:1024], r2v))
        ev.put(f"p2:r2bf:{c}", vec)
        ev.wait(tens, f"p2:r2bf:{c}")
        ins = nc.tensor.matmul(ps_av[:, 0:512], ones_r[:], rows2bf[0:1, 0:512],
                               start=True, stop=True)
        tens.inc(ins)
        ins = nc.tensor.matmul(ps_sc[:, 0:512], ones_r[:], rows2bf[0:1, 512:1024],
                               start=True, stop=True)
        tens.inc(ins)
        ev.put(f"p2:m2bc:{c}", tens)
        ev.wait(vec, f"p2:m2bc:{c}")
        vec.inc(nc.vector.tensor_copy(m2b[:], ps_av[:, 0:512]))
        vec.inc(nc.vector.tensor_copy(s2b[:], ps_sc[:, 0:512]))
        ev.put(f"p2:m2b:{c}", vec)
        vec.wait(sV, vec.n)
        for n in range(NT):
            vself(nc.vector.tensor_tensor(x1bf[n % 2][:], x1c[:, n * 512:(n + 1) * 512],
                                          m2b[:], ALU.subtract))
            if ev.has(f"p2:ln2st:{c - 1}:{n}"):
                ev.wait(vec, f"p2:ln2st:{c - 1}:{n}")
            if ev.has(f"p2:lg:{c - 1}"):
                ev.wait(vec, f"p2:lg:{c - 1}")
            vec.inc(nc.vector.tensor_tensor(ln2c[:, n * 512:(n + 1) * 512],
                                            x1bf[n % 2][:], s2b[:], ALU.mult))
            ev.put(f"p2:ln2:{c}:{n}", vec)
            pending_stores.append(("ln2", c, n))
        ev.put(f"p2:ln2:{c}", vec)

        last_pexp[0] = f"p2:m2b:{c}"

    emit_router(CH - 1)
    flush_stores()

    if debug:
        for n in range(NT):
            ev.wait(syn, f"p2:ln2st:{CH - 1}:{n}")
        syn.inc(nc.sync.dma_start(
            d_ln2.rearrange("p (k t) -> p k t", t=T),
            ln2_s.rearrange("(k p) t -> p k t", p=P)), 16)
        ev.wait(syn, f"p2:gatst:{CH - 1}")
        syn.inc(nc.sync.dma_start(d_gat, gat_s), 16)

    p2_done = [f"p2:gatst:{CH - 1}", f"p2:ln2st:{CH - 1}:{NT - 1}", f"p2:ln2:{CH - 1}", f"p2:gtr:{CH - 1}",
               f"p2:x1st:{CH - 1}:{NT - 1}", f"p2:wo:{CH - 1}:{NT - 1}",
               f"p2:vc:{CH - 1}:{H - 1}"]
    st2.close()
    ps2.close()

    st3 = ExitStack()
    sb3 = lambda name, shape, dt: st3.enter_context(nc.sbuf_tensor(name, shape, dt))
    ln2f = sb3("ln2f", [P, KT * T], BF16)
    u_sb = sb3("u_sb", [P, KT * T], BF16)
    growall = sb3("growall", [1, E * T], BF16)
    gate_b = sb3("gate_b", [P, T], F32)
    h_sb = [sb3(f"h_sb{i}", [P, T], F32) for i in range(2)]
    gtmp = [sb3(f"gtmp{i}", [P, T], BF16) for i in range(2)]
    ps3 = ExitStack()
    ps3a = ps3.enter_context(nc.psum_tensor("ps3a", [P, T], F32))
    ps3b = ps3.enter_context(nc.psum_tensor("ps3b", [P, T], F32))

    for qq in (tens, vec, act, syn):
        for e in p2_done:
            ev.wait(qq, e)

    for k in range(KT):
        if k % 2 == 0:
            syn.inc(nc.sync.dma_start(ln2f[:, k * T:(k + 1) * T], ln2_s[k * P:(k + 1) * P, :]), 16)
            ev.put(f"p3:ln2f{k}", syn)
        else:
            act.inc(nc.scalar.dma_start(ln2f[:, k * T:(k + 1) * T], ln2_s[k * P:(k + 1) * P, :]), 16)
            ev.put(f"p3:ln2f{k}", act)

    syn.inc(nc.sync.dma_start(growall[:], gat_s.rearrange("e t -> (e t)").rearrange("(o f) -> o f", o=1)), 16)
    ev.put("p3:growall", syn)

    def emit_gb(e):
        ev.wait(tens, "p3:growall")
        if e > 0:
            ev.wait(tens, f"p3:h:{e - 1}:4")
        for c in range(CH):
            ins = nc.tensor.matmul(ps3a[:, c * 512:(c + 1) * 512], ones_r[:],
                                   growall[0:1, e * T + c * 512: e * T + (c + 1) * 512],
                                   start=True, stop=True)
        tens.inc(ins)
        ev.put(f"p3:gb:{e}", tens)
        ev.wait(act, f"p3:gb:{e}")
        if e > 0:
            ev.wait(act, f"p3:u:{e - 1}:{NT - 1}")
        act.inc(nc.scalar.copy(gate_b[:], ps3a[:]))
        ev.put(f"p3:gbe:{e}", act)

    def emit_acc(e, n):
        ev.wait(gps, f"p3:h:{e}:{n}")
        if ev.has(f"p3:acc:{e - 1}:{n}"):
            ev.wait(gps, f"p3:acc:{e - 1}:{n}")
        else:
            ev.wait(gps, f"p2:x1st:3:{n}")
        gps.inc(nc.gpsimd.dma_start(outT[n * P:(n + 1) * P, :], h_sb[n % 2][:],
                                    accum_op=ALU.add), 16)
        ev.put(f"p3:acc:{e}:{n}", gps)

    emit_gb(0)
    for e in range(E):
        for n in range(NT):
            wkey = f"p3:uw:{e}:{n}"
            if ev.has(f"p3:umm:{e}:{n - 2}"):
                ev.wait(gps, f"p3:umm:{e}:{n - 2}")
            elif e > 0 and n < 2:
                ev.wait(gps, f"p3:dmm:{e - 1}:{NT - 2 + n}")
            else:
                ev.wait(gps, f"p2:op:{CH - 1}:{NT - 1}")
            gps.inc(nc.gpsimd.dma_start(wseg[n % 2][:], wu[e, n].rearrange("p k j -> p (k j)")), 16)
            ev.put(wkey, gps)
            if e > 0 and n < 2:
                emit_acc(e - 1, NT - 2 + n)
            pp = ps3a if n % 2 == 0 else ps3b
            ev.wait(tens, wkey)
            ev.wait(tens, f"p3:gbe:{e}")
            if ev.has(f"p3:u:{e}:{n - 2}"):
                ev.wait(tens, f"p3:u:{e}:{n - 2}")
            elif e > 0 and n < 2:
                ev.wait(tens, f"p3:h:{e - 1}:{NT - 2 + n}")
            for k in range(KT):
                ev.wait(tens, f"p3:ln2f{k}")
                for c in range(CH):
                    ins = nc.tensor.matmul(pp[:, c * 512:(c + 1) * 512],
                                           wseg[n % 2][:, k * P:(k + 1) * P],
                                           ln2f[:, k * T + c * 512: k * T + (c + 1) * 512],
                                           start=(k == 0), stop=(k == KT - 1))
            tens.inc(ins)
            ev.put(f"p3:umm:{e}:{n}", tens)
            ev.wait(vec, f"p3:umm:{e}:{n}")
            ev.wait(vec, f"p3:gbe:{e}")
            for c in range(CH):
                vec.inc(nc.vector.scalar_tensor_tensor(
                    u_sb[:, n * T + c * 512: n * T + (c + 1) * 512],
                    gate_b[:, c * 512:(c + 1) * 512], 1.0,
                    pp[:, c * 512:(c + 1) * 512], ALU.mult, ALU.mult))
            ev.put(f"p3:u:{e}:{n}", vec)

        for n in range(NT):
            wkey = f"p3:gw:{e}:{n}"
            if ev.has(f"p3:gmm:{e}:{n - 2}"):
                ev.wait(gps, f"p3:gmm:{e}:{n - 2}")
            else:
                ev.wait(gps, f"p3:umm:{e}:{NT - 2 + n}")
            gps.inc(nc.gpsimd.dma_start(wseg[n % 2][:], wg[e, n].rearrange("p k j -> p (k j)")), 16)
            ev.put(wkey, gps)
            pp = ps3a if n % 2 == 0 else ps3b
            ev.wait(tens, wkey)
            if ev.has(f"p3:sil:{e}:{n - 2}"):
                ev.wait(tens, f"p3:sil:{e}:{n - 2}")
            else:
                ev.wait(tens, f"p3:u:{e}:{NT - 2 + n}")
            for k in range(KT):
                for c in range(CH):
                    ins = nc.tensor.matmul(pp[:, c * 512:(c + 1) * 512],
                                           wseg[n % 2][:, k * P:(k + 1) * P],
                                           ln2f[:, k * T + c * 512: k * T + (c + 1) * 512],
                                           start=(k == 0), stop=(k == KT - 1))
            tens.inc(ins)
            ev.put(f"p3:gmm:{e}:{n}", tens)
            ev.wait(act, f"p3:gmm:{e}:{n}")
            if ev.has(f"p3:prod:{e}:{n - 2}"):
                ev.wait(act, f"p3:prod:{e}:{n - 2}")
            ins = nc.scalar.activation(gtmp[n % 2][:], pp[:], AF.Silu)
            act.inc(ins)
            ev.put(f"p3:sil:{e}:{n}", act)
            ev.wait(vec, f"p3:sil:{e}:{n}")
            vself(nc.vector.tensor_scalar_min(gtmp[n % 2][:], gtmp[n % 2][:], 7.0))
            ev.put(f"p3:clip:{e}:{n}", vec)
            ev.wait(vec, f"p3:u:{e}:{n}")
            vec.inc(nc.vector.tensor_tensor(u_sb[:, n * T:(n + 1) * T], gtmp[n % 2][:],
                                            u_sb[:, n * T:(n + 1) * T], ALU.mult))
            ev.put(f"p3:prod:{e}:{n}", vec)
        ev.put(f"p3:prod:{e}", vec)

        for n in range(NT):
            wkey = f"p3:dw:{e}:{n}"
            if ev.has(f"p3:dmm:{e}:{n - 2}"):
                ev.wait(gps, f"p3:dmm:{e}:{n - 2}")
            else:
                ev.wait(gps, f"p3:gmm:{e}:{NT - 2 + n}")
            gps.inc(nc.gpsimd.dma_start(wseg[n % 2][:], wd[e, n].rearrange("p k j -> p (k j)")), 16)
            ev.put(wkey, gps)
            if n == 6 and e + 1 < E:
                emit_gb(e + 1)
            pp = ps3a if n % 2 == 0 else ps3b
            ev.wait(tens, wkey)
            ev.wait(tens, f"p3:prod:{e}")
            if n == 6 and e + 1 < E:
                ev.wait(tens, f"p3:gbe:{e + 1}")
            if ev.has(f"p3:h:{e}:{n - 2}"):
                ev.wait(tens, f"p3:h:{e}:{n - 2}")
            else:
                ev.wait(tens, f"p3:sil:{e}:{NT - 2 + n}")
            for k in range(KT):
                for c in range(CH):
                    ins = nc.tensor.matmul(pp[:, c * 512:(c + 1) * 512],
                                           wseg[n % 2][:, k * P:(k + 1) * P],
                                           u_sb[:, k * T + c * 512: k * T + (c + 1) * 512],
                                           start=(k == 0), stop=(k == KT - 1))
            tens.inc(ins)
            ev.put(f"p3:dmm:{e}:{n}", tens)
            if n >= 2:
                emit_acc(e, n - 2)
            ev.wait(vec, f"p3:dmm:{e}:{n}")
            if ev.has(f"p3:acc:{e}:{n - 2}"):
                ev.wait(vec, f"p3:acc:{e}:{n - 2}")
            elif ev.has(f"p3:acc:{e - 1}:{NT - 2 + n}"):
                ev.wait(vec, f"p3:acc:{e - 1}:{NT - 2 + n}")
            vec.inc(nc.vector.tensor_copy(h_sb[n % 2][:], pp[:]))
            ev.put(f"p3:h:{e}:{n}", vec)
        if e == E - 1:
            emit_acc(e, NT - 2)
            emit_acc(e, NT - 1)

    gps.wait(sG, gps.n)
    syn.wait(sS, syn.n)
    for n in range(NT):
        ev.wait(syn, f"p3:acc:{E - 1}:{n}")

    st3.close()
    ps3.close()
    st23.close()
    st.close()
    return nc


def _prep_weights(d):
    def re4(W):
        return np.ascontiguousarray(W.reshape(KT, P, NT, P).transpose(2, 1, 0, 3))

    def re5(W):
        return np.ascontiguousarray(W.reshape(E, KT, P, NT, P).transpose(0, 3, 2, 1, 4))

    def col(b):
        return np.ascontiguousarray(b.reshape(NT, P).T)

    import ml_dtypes
    identb = np.eye(P, dtype=ml_dtypes.bfloat16)
    identf = np.eye(P, dtype=np.float32)
    ii = np.arange(P)
    maskd = np.where(ii[None, :] <= ii[:, None], np.float32(0), np.float32(NEG)).astype(np.float32)
    return {
        "wq": re4(d["L0_attn_w_q"]), "wk": re4(d["L0_attn_w_k"]),
        "wv": np.ascontiguousarray(d["L0_attn_w_v"].reshape(KT, P, D)),
        "wo": re4(d["L0_attn_w_o"]),
        "bq": col(d["L0_attn_b_q"]), "bk": col(d["L0_attn_b_k"]),
        "bv": col(d["L0_attn_b_v"]), "bo": col(d["L0_attn_b_o"]),
        "wr": np.ascontiguousarray(d["L0_ffn_Wr"].reshape(KT, P, 8).transpose(1, 0, 2).reshape(P, P)),
        "wg": re5(d["L0_ffn_Wg"]), "wu": re5(d["L0_ffn_Wu"]), "wd": re5(d["L0_ffn_Wd"]),
        "identb": identb, "identf": identf, "maskd": maskd,
    }


def kernel(**inputs):
    x = np.asarray(inputs["L0_attn_query"], dtype=np.float32)
    shared = _prep_weights(inputs)
    nc = build()
    in_maps = []
    for c in range(8):
        m = dict(shared)
        m["xT"] = np.ascontiguousarray(x[c].T)
        in_maps.append(m)
    res = run_bass_kernel_spmd(nc, in_maps, core_ids=list(range(8)))
    out = np.empty((8, T, D), dtype=np.float32)
    for c in range(8):
        out[c] = np.asarray(res.results[c]["outT"]).T
    return out
